# version 23
# baseline (speedup 1.0000x reference)
import numpy as np
from contextlib import ExitStack

import concourse.bass as bass
import concourse.mybir as mybir
from concourse.bass_utils import run_bass_kernel_spmd

F32 = mybir.dt.float32
I32 = mybir.dt.int32
BF16 = mybir.dt.bfloat16
AF = mybir.ActivationFunctionType
ALU = mybir.AluOpType
AX = mybir.AxisListType

ENGS = ("PE", "DVE", "ACT", "POOL", "SP")
_P2SUB = 99
_NHEADS = 99
_NCHDBG = 4


class Prog:

    def __init__(self, nc, ctx):
        self.nc = nc
        self.ctx = ctx
        self.eng = {"PE": nc.tensor, "DVE": nc.vector, "ACT": nc.scalar,
                    "POOL": nc.gpsimd, "SP": nc.sync}
        self.engsem = {}
        self.pos = {}
        for e in ("PE", "DVE", "ACT", "POOL"):
            self.engsem[e] = ctx.enter_context(nc.semaphore("qs_" + e))
            self.pos[e] = 0
        self.streams = {}
        self.satisfied = {e: {} for e in ENGS}
        self.res = {}

    def stream(self, name, k=2):
        if name not in self.streams:
            sems = []
            for i in range(k):
                h = self.ctx.enter_context(self.nc.semaphore(f"dm_{name}{i}"))
                sems.append([f"dm_{name}{i}", h, 0])
            self.streams[name] = {"sems": sems, "rr": 0}
        return self.streams[name]

    def _r(self, key):
        if key not in self.res:
            self.res[key] = {"w": None, "r": []}
        return self.res[key]

    def _wait(self, ename, tokens):
        best = {}
        for (sname, h, v) in tokens:
            cur = best.get(sname)
            if cur is None or v > cur[1]:
                best[sname] = (h, v)
        q = self.eng[ename]
        for sname in sorted(best):
            h, v = best[sname]
            if ename == "PE" and sname == "qs_PE":
                continue
            if v <= self.satisfied[ename].get(sname, -1):
                continue
            q.wait_ge(h, v)
            self.satisfied[ename][sname] = v

    def _deps(self, reads, writes):
        toks = []
        for k in reads:
            r = self._r(k)
            if r["w"] is not None:
                toks.append(r["w"])
        for k in writes:
            r = self._r(k)
            if r["w"] is not None:
                toks.append(r["w"])
            toks.extend(r["r"])
        return toks

    def _commit(self, tok, reads, writes):
        for k in writes:
            r = self._r(k)
            r["w"] = tok
            r["r"] = []
        for k in reads:
            if k not in writes:
                self._r(k)["r"].append(tok)

    def emit(self, ename, fn, reads=(), writes=()):
        self._wait(ename, self._deps(reads, writes))
        instr = fn(self.eng[ename])
        self.pos[ename] += 1
        instr.then_inc(self.engsem[ename], 1)
        self._commit(("qs_" + ename, self.engsem[ename], self.pos[ename]),
                     reads, writes)

    def group(self, ename, fns, reads=(), writes=()):
        self._wait(ename, self._deps(reads, writes))
        instr = None
        for fn in fns:
            instr = fn(self.eng[ename])
        self.pos[ename] += 1
        instr.then_inc(self.engsem[ename], 1)
        self._commit(("qs_" + ename, self.engsem[ename], self.pos[ename]),
                     reads, writes)

    def dma(self, ename, stream_name, out, in_, reads=(), writes=(), k=2, **kw):
        st = self.stream(stream_name, k)
        slot = st["sems"][st["rr"] % len(st["sems"])]
        st["rr"] += 1
        toks = self._deps(reads, writes)
        if slot[2] > 0:
            toks.append((slot[0], slot[1], slot[2]))
        self._wait(ename, toks)
        instr = self.eng[ename].dma_start(out=out, in_=in_, **kw)
        slot[2] += 16
        instr.then_inc(slot[1], 16)
        self._commit((slot[0], slot[1], slot[2]), reads, writes)

    def barrier(self):
        marks = [("qs_" + e, self.engsem[e], self.pos[e])
                 for e in ("PE", "DVE", "ACT", "POOL")]
        for st in self.streams.values():
            for sname, h, tot in st["sems"]:
                if tot > 0:
                    marks.append((sname, h, tot))
        for e in ENGS:
            self._wait(e, marks)


def build_program(TW=2048, PFX=512):
    D = 1024
    H8 = 8
    C = 64
    FF = 4096
    DC = D // 128
    TA = TW + PFX
    NCA = TA // C
    NCP = PFX // C
    NTA = TA // 128
    NTP = PFX // 128
    PT = TW // 128
    FC = FF // 128

    nc = bass.Bass("TRN2", target_bir_lowering=False)

    dxdm = nc.dram_tensor("xdm", [D, TA], F32, kind="ExternalInput").ap()
    dxtm = nc.dram_tensor("xtm", [TW, D], F32, kind="ExternalInput").ap()
    dwq = nc.dram_tensor("wq", [D, H8 * 128], F32, kind="ExternalInput").ap()
    dwk = nc.dram_tensor("wk", [D, H8 * 128], F32, kind="ExternalInput").ap()
    dwg = nc.dram_tensor("wg", [D, H8 * 128], F32, kind="ExternalInput").ap()
    dwv = nc.dram_tensor("wv", [D, H8 * 128], F32, kind="ExternalInput").ap()
    dwo = nc.dram_tensor("wo", [H8 * 128, D], F32, kind="ExternalInput").ap()
    dw1t = nc.dram_tensor("w1t", [D, FF], F32, kind="ExternalInput").ap()
    dw2t = nc.dram_tensor("w2t", [FF, D], F32, kind="ExternalInput").ap()
    db1 = nc.dram_tensor("b1", [128, FF], F32, kind="ExternalInput").ap()
    db2 = nc.dram_tensor("b2", [128, D], F32, kind="ExternalInput").ap()
    diden = nc.dram_tensor("iden", [128, 128], BF16, kind="ExternalInput").ap()
    dtri = nc.dram_tensor("tri", [128, C], BF16, kind="ExternalInput").ap()
    dones = nc.dram_tensor("onescol", [128, 1], BF16, kind="ExternalInput").ap()
    didenf = nc.dram_tensor("idenf", [128, 128], F32, kind="ExternalInput").ap()
    dout = nc.dram_tensor("out", [TW, D], F32, kind="ExternalOutput").ap()

    d_o = nc.dram_tensor("od", [H8 * 128, TW], BF16, kind="Internal").ap()
    d_yscr = nc.dram_tensor("yscr", [TW, D], BF16, kind="Internal").ap()
    d_n1scr = nc.dram_tensor("n1scr", [DC * PT * 128, 128], BF16, kind="Internal").ap()
    d_qscr = nc.dram_tensor("qscr", [H8 * PT * 128, 128], F32, kind="Internal").ap()

    with ExitStack() as ctx:
        e = ctx.enter_context
        biga = e(nc.sbuf_tensor("biga", [128, 32768], BF16))
        bigb = e(nc.sbuf_tensor("bigb", [128, 32768], BF16))
        wsl = e(nc.sbuf_tensor("wsl", [128, 16384], BF16))
        lneg = e(nc.sbuf_tensor("lneg", [128, 4096], F32))
        cuml = e(nc.sbuf_tensor("cuml", [128, 4096], F32))
        remz = e(nc.sbuf_tensor("remz", [128, 1024], F32))
        yf32 = e(nc.sbuf_tensor("yf32", [128, 1024], F32))
        u1b = e(nc.sbuf_tensor("u1b", [128, 1024], F32))
        pcf = e(nc.sbuf_tensor("pcf", [128, NCA], F32))
        nrm = e(nc.sbuf_tensor("nrm", [128, 8 + 3 * NTA], F32))
        sct = e(nc.sbuf_tensor("sct", [128, 64], F32))
        scp = e(nc.sbuf_tensor("scp", [128, 24], F32))
        wmp = e(nc.sbuf_tensor("wmp", [128, 18], F32))
        wrow = e(nc.sbuf_tensor("wrow", [1, 12], F32))
        khts = e(nc.sbuf_tensor("khts", [128, 128], BF16))
        ssb = e(nc.sbuf_tensor("ssb", [128, 256], F32))
        sbf = e(nc.sbuf_tensor("sbf", [128, 128], BF16))
        ostg = e(nc.sbuf_tensor("ostg", [128, 128], BF16))
        amsk = e(nc.sbuf_tensor("amsk", [128, 64], BF16))
        c_iden = e(nc.sbuf_tensor("c_iden", [128, 128], BF16))
        c_idenf = e(nc.sbuf_tensor("c_idenf", [128, 128], F32))
        c_tri = e(nc.sbuf_tensor("c_tri", [128, C], BF16))
        c_ones = e(nc.sbuf_tensor("c_ones", [128, 1], BF16))
        cstv = e(nc.sbuf_tensor("cstv", [128, 3], F32))

        ps = [e(nc.psum_tensor(f"ps{i}", [128, 512], F32)) for i in range(6)]
        pt6 = e(nc.psum_tensor("pt6", [128, 1024], BF16))
        pt7 = e(nc.psum_tensor("pt7", [128, 1024], BF16))

        block = e(nc.Block())
        P = Prog(nc, ctx)

        vt = biga[:, 0:NTA * 1024]
        qstg = biga[:, NTA * 1024:NTA * 1024 + 2048].bitcast(F32)
        XB0 = NTA * 1024 + 2048
        assert XB0 + 5 * TA <= 32768 and 8192 + 3 * TA <= 16384
        xres = []
        for c in range(8):
            if c >= 3:
                xres.append(biga[:, XB0 + (c - 3) * TA:XB0 + (c - 2) * TA])
            else:
                xres.append(wsl[:, 8192 + c * TA:8192 + (c + 1) * TA])
        XR = [f"xr{c}" for c in range(8)]
        t2q = biga
        F_k = bigb[:, 0:2 * TA].bitcast(F32)
        F_lng = bigb[:, 2 * TA:2 * TA + 1024].bitcast(F32)
        B1 = 2 * TA + 1024
        F_cl = bigb[:, B1:B1 + 2 * TA].bitcast(F32)
        F_e1 = bigb[:, B1 + 2 * TA:B1 + 4 * TA].bitcast(F32)
        F_e2 = bigb[:, B1 + 4 * TA:B1 + 6 * TA].bitcast(F32)
        B0 = B1 + 6 * TA
        A_ksq = bigb[:, B0:B0 + TA]
        A_khP = [bigb[:, B0 + TA:B0 + 2 * TA],
                 bigb[:, B0 + 2 * TA:B0 + 3 * TA]]
        A_qhP = [bigb[:, B0 + 3 * TA:B0 + 3 * TA + TW],
                 bigb[:, B0 + 3 * TA + TW:B0 + 3 * TA + 2 * TW]]
        assert B0 + 3 * TA + 2 * TW <= 32768
        t1q = bigb
        WQf = wsl[:, 0:8192]
        WVf = wsl[:, 0:8192]
        WKh = wsl[:, 0:1024]
        WGh = wsl[:, 1024:2048]
        WO = wsl[:, 0:8192]
        OTL = wsl[:, 8192:12288]
        N1S = wsl[:, 12288:14336]
        b2b = wsl[:, 14336:15360]
        N2TM = wsl[:, 0:4096]
        N2F = wsl[:, 4096:8192]
        N1D = wsl[:, 8192:10240]
        YST = wsl[:, 10240:12288]

        P.emit("DVE", lambda q: q.memset(cstv[:, 0:1], 1e-5), writes=["cstv"])
        P.emit("DVE", lambda q: q.memset(cstv[:, 1:2], 1.0 / D), writes=["cstv"])
        P.emit("DVE", lambda q: q.memset(cstv[:, 2:3], 1.0 / FF), writes=["cstv"])
        P.dma("SP", "consts", c_iden[:], diden[:], ["d_iden"], ["c_iden"], k=4)
        P.dma("SP", "consts", c_idenf[:], didenf[:], ["d_idenf"], ["c_idenf"], k=4)
        P.dma("SP", "consts", c_tri[:], dtri[:], ["d_tri"], ["c_tri"], k=4)
        P.dma("SP", "consts", c_ones[:], dones[:], ["d_ones"], ["c_ones"], k=4)
        P.dma("POOL", "wload", WVf.rearrange("p (c n) -> p c n", n=1024),
              dwv.rearrange("(c p) n -> p c n", p=128),
              ["d_wv"], ["sl_wv", "sl_wq"], k=4)
        for c in range(8):
            P.dma("POOL", "xload", xres[c][:],
                  dxdm.rearrange("(c p) tt -> p c tt", p=128)[:, c, :],
                  ["d_xdm"], [f"xr{c}"], k=4)

        for t in range(NTA):
            for half in range(2):
                mv = [(lambda q, c=c, t=t, half=half: q.matmul(
                    ps[half][:, 0:512], xres[c][:, t * 128:(t + 1) * 128],
                    WVf[:, c * 1024 + half * 512:c * 1024 + half * 512 + 512],
                    start=(c == 0), stop=(c == DC - 1))) for c in range(DC)]
                P.group("PE", mv, reads=XR + ["sl_wv"], writes=[f"ps{half}"])
                P.emit("ACT", lambda q, t=t, half=half: q.activation(
                    vt[:, t * 1024 + half * 512:t * 1024 + half * 512 + 512],
                    ps[half][:, 0:512], AF.Copy, bias=0.0, scale=1.0),
                    reads=[f"ps{half}"], writes=["vt"])
        P.dma("POOL", "wload", WQf.rearrange("p (c n) -> p c n", n=1024),
              dwq.rearrange("(c p) n -> p c n", p=128),
              ["d_wq"], ["sl_wq", "sl_wv"], k=4)
        def q_proj(t):
            qslot = t % 2
            qa = lneg[:, qslot * 1024:qslot * 1024 + 1024]
            for half in range(2):
                mq = [(lambda q, c=c, t=t, half=half: q.matmul(
                    ps[2 + half][:, 0:512], xres[c][:, t * 128:(t + 1) * 128],
                    WQf[:, c * 1024 + half * 512:c * 1024 + half * 512 + 512],
                    start=(c == 0), stop=(c == DC - 1))) for c in range(DC)]
                P.group("PE", mq, reads=XR + ["sl_wq"], writes=[f"ps{2 + half}"])
                P.emit("ACT", lambda q, qa=qa, half=half: q.activation(
                    qa[:, half * 512:half * 512 + 512], ps[2 + half][:, 0:512],
                    AF.Copy, bias=0.0, scale=1.0),
                    reads=[f"ps{2 + half}"], writes=[f"qtm{qslot}"])

        def q_tail(t):
            tw = t - NTP
            qslot = t % 2
            qa = lneg[:, qslot * 1024:qslot * 1024 + 1024]
            sc0 = qslot * 24
            nr0 = qslot * 8
            sk, nk = f"sct{qslot}", f"nrmq{qslot}"
            for h8 in range(H8):
                P.emit("DVE", lambda q, qa=qa, h8=h8: q.scalar_tensor_tensor(
                    u1b[:, 0:128], qa[:, h8 * 128:(h8 + 1) * 128], 1.0,
                    qa[:, h8 * 128:(h8 + 1) * 128], ALU.bypass, ALU.mult,
                    accum_out=nrm[:, nr0 + h8:nr0 + h8 + 1]),
                    reads=[f"qtm{qslot}"], writes=["u1b", nk])
            P.emit("ACT", lambda q: q.activation(
                sct[:, sc0:sc0 + H8], nrm[:, nr0:nr0 + H8], AF.Sqrt,
                bias=0.0, scale=1.0), reads=[nk], writes=[sk])
            P.emit("DVE", lambda q: q.tensor_scalar(
                sct[:, sc0 + 8:sc0 + 8 + H8], sct[:, sc0:sc0 + H8],
                1e-12, None, ALU.max), reads=[sk], writes=[sk])
            P.emit("DVE", lambda q: q.reciprocal(
                sct[:, sc0 + 16:sc0 + 16 + H8], sct[:, sc0 + 8:sc0 + 8 + H8]),
                reads=[sk], writes=[sk])
            qb = cuml[:, qslot * 1024:qslot * 1024 + 1024]
            for bb in range(2):
                pv = ps[bb]
                for k4 in range(4):
                    h8 = bb * 4 + k4
                    qh_t = qb[:, h8 * 128:(h8 + 1) * 128]
                    P.emit("DVE", lambda q, qa=qa, h8=h8, qh_t=qh_t, sc0=sc0:
                           q.tensor_scalar(
                               qh_t, qa[:, h8 * 128:(h8 + 1) * 128],
                               sct[:, sc0 + 16 + h8:sc0 + 17 + h8], None,
                               ALU.mult),
                           reads=[f"qtm{qslot}", sk],
                           writes=[f"qhatb{qslot}"])
                    P.emit("PE", lambda q, qh_t=qh_t, k4=k4, pv=pv: q.transpose(
                        pv[:, k4 * 128:(k4 + 1) * 128], qh_t, c_idenf[:]),
                        reads=[f"qhatb{qslot}", "c_idenf"], writes=[f"ps{bb}"])
                qsg = qstg[:, bb * 512:bb * 512 + 512]
                P.emit("DVE", lambda q, pv=pv, qsg=qsg: q.tensor_copy(
                    qsg, pv[:, 0:512]),
                    reads=[f"ps{bb}"], writes=[f"qstg{bb}"])
                P.dma("SP", "qstore",
                      d_qscr.rearrange("(h t p) n -> t p h n", h=H8, p=128)
                      [tw, :, bb * 4:bb * 4 + 4, :],
                      qsg.rearrange("p (h n) -> p h n", h=4),
                      [f"qstg{bb}"], ["d_qscr"], k=4)

        q_proj(NTP)
        for t in range(NTP, NTA):
            if t + 1 < NTA:
                q_proj(t + 1)
            q_tail(t)

        mean_jobs = []
        for wi_, dsrc_ in enumerate((dw1t, dw2t)):
            for cch_ in range(8):
                mean_jobs.append((wi_, cch_, dsrc_))

        def _emit_mean_chunk():
            if not mean_jobs:
                return
            wi, cch, dsrc = mean_jobs.pop(0)
            slb = (wi * 8 + cch) % 2
            buf = lneg if slb == 0 else cuml
            bkey = "lneg" if slb == 0 else "cuml"
            alias = ["qtm0", "qtm1"] if slb == 0 else ["qhatb"]
            if wi == 0:
                wsrc = dsrc.rearrange("(c p) n -> p c n", p=128)[:, cch, :]
                dst = buf[:]
            else:
                wsrc = dsrc.rearrange("(c p) n -> p c n", p=128)                     [:, cch * 4:cch * 4 + 4, :]
                dst = buf[:].rearrange("p (c n) -> p c n", c=4)
            P.dma("SP", "wmean", dst, wsrc, [f"d_w{wi + 1}"], [bkey] + alias)
            P.emit("DVE", lambda q, buf=buf, wi=wi, cch=cch: q.tensor_reduce(
                wmp[:, wi * 8 + cch:wi * 8 + cch + 1], buf[:], AX.X, ALU.add,
                apply_absolute_value=True),
                reads=[bkey], writes=["wmp"])

        prep_q = []
        NHD = min(H8, _NHEADS)

        def build_prep(h8):
            par = h8 % 2
            kh_a, qh_a = A_khP[par], A_qhP[par]
            kkey, qkey = f"khb{par}", f"qhb{par}"
            rks = sct[:, 24 + par * NTA:24 + (par + 1) * NTA]
            pcf_a = pcf[:, 0:NCA] if par == 0 else nrm[:, 8:8 + NCA]
            pkey = f"pcf{par}"
            E = prep_q.append
            E(lambda h8=h8: P.dma(
                "POOL", "wkh", WKh.rearrange("p (c n) -> p c n", n=128),
                dwk.rearrange("(c p) n -> p c n", p=128)
                [:, :, h8 * 128:(h8 + 1) * 128],
                ["d_wk"], ["wkh", "sl_wq"], k=2))
            E(lambda h8=h8: P.dma(
                "POOL", "wgh", WGh.rearrange("p (c n) -> p c n", n=128),
                dwg.rearrange("(c p) n -> p c n", p=128)
                [:, :, h8 * 128:(h8 + 1) * 128],
                ["d_wg"], ["wgh", "sl_wq"], k=2))
            for s0 in range(0, TA, 512):
                gw = min(512, TA - s0)
                sl = slice(s0, s0 + gw)
                mk = [(lambda q, c=c, s0=s0, gw=gw: q.matmul(
                    ps[0][:, 0:gw], WKh[:, c * 128:(c + 1) * 128],
                    xres[c][:, s0:s0 + gw],
                    start=(c == 0), stop=(c == DC - 1))) for c in range(DC)]
                mg = [(lambda q, c=c, s0=s0, gw=gw: q.matmul(
                    ps[1][:, 0:gw], WGh[:, c * 128:(c + 1) * 128],
                    xres[c][:, s0:s0 + gw],
                    start=(c == 0), stop=(c == DC - 1))) for c in range(DC)]
                E(lambda mk=mk: P.group("PE", mk, reads=XR + ["wkh"],
                                        writes=["ps0"]))
                E(lambda sl=sl, gw=gw: P.emit("ACT", lambda q: q.activation(
                    F_k[:, sl], ps[0][:, 0:gw], AF.Copy, bias=0.0, scale=1.0),
                    reads=["ps0"], writes=["kdk"]))
                E(lambda sl=sl, gw=gw: P.emit("ACT", lambda q: q.activation(
                    A_ksq[:, sl], ps[0][:, 0:gw], AF.Square, bias=0.0,
                    scale=1.0), reads=["ps0"], writes=["ksq"]))
                E(lambda mg=mg: P.group("PE", mg, reads=XR + ["wgh"],
                                        writes=["ps1"]))
                E(lambda gw=gw: P.emit("ACT", lambda q: q.activation(
                    remz[:, 0:gw], ps[1][:, 0:gw], AF.Exp, bias=0.0,
                    scale=-1.0), reads=["ps1"], writes=["remz"]))
                E(lambda gw=gw: P.emit("ACT", lambda q: q.activation(
                    F_lng[:, 0:gw], remz[:, 0:gw], AF.Ln, bias=1.0,
                    scale=1.0), reads=["remz"], writes=["lng"]))
                for cc in range(gw // C):
                    ci0 = s0 // C + cc
                    E(lambda cc=cc, ci0=ci0: P.emit(
                        "DVE", lambda q: q.tensor_tensor_scan(
                            F_cl[:, ci0 * C:(ci0 + 1) * C],
                            F_lng[:, cc * C:(cc + 1) * C],
                            F_lng[:, cc * C:(cc + 1) * C],
                            0.0, ALU.add, ALU.bypass),
                        reads=["lng"], writes=["cuml", "qhat"]))
            nmm = [(lambda q, t2=t2: q.matmul(
                ps[0][:, t2:t2 + 1], A_ksq[:, t2 * 128:(t2 + 1) * 128],
                c_ones[:], start=True, stop=True,
                skip_group_check=True)) for t2 in range(NTA)]
            E(lambda nmm=nmm: P.group("PE", nmm, reads=["ksq", "c_ones"],
                                      writes=["ps0"]))
            E(lambda: P.emit("ACT", lambda q: q.activation(
                nrm[:, 8 + NCA:8 + NCA + NTA], ps[0][:, 0:NTA], AF.Sqrt,
                bias=0.0, scale=1.0), reads=["ps0"],
                writes=["nrmk", "nrmq1"]))
            E(lambda: P.emit("DVE", lambda q: q.tensor_scalar(
                nrm[:, 8 + NCA:8 + NCA + NTA], nrm[:, 8 + NCA:8 + NCA + NTA],
                1e-12, None, ALU.max), reads=["nrmk"], writes=["nrmk"]))
            E(lambda rks=rks, par=par: P.emit("DVE", lambda q: q.reciprocal(
                rks, nrm[:, 8 + NCA:8 + NCA + NTA]),
                reads=["nrmk"], writes=[f"rk{par}", "sct0", "sct1"]))
            E(lambda pcf_a=pcf_a, pkey=pkey: P.emit(
                "ACT", lambda q: q.activation(
                    pcf_a, F_cl[:, C - 1:TA:C], AF.Exp, bias=0.0, scale=-1.0),
                reads=["cuml"], writes=[pkey, "nrmq1"]))
            for s0 in range(0, TA, 512):
                sl = slice(s0, min(s0 + 512, TA))
                E(lambda sl=sl: P.emit("ACT", lambda q: q.activation(
                    F_e1[:, sl], F_cl[:, sl], AF.Exp, bias=0.0, scale=-1.0),
                    reads=["cuml"], writes=["e1"]))
                E(lambda sl=sl: P.emit("ACT", lambda q: q.activation(
                    F_e2[:, sl], F_cl[:, sl], AF.Exp, bias=0.0, scale=1.0),
                    reads=["cuml"], writes=["e2"]))
            for s0 in range(0, TA, 512):
                sl = slice(s0, min(s0 + 512, TA))
                E(lambda sl=sl, kh_a=kh_a, kkey=kkey: P.emit(
                    "POOL", lambda q: q.tensor_tensor(
                        kh_a[:, sl], F_k[:, sl], F_e2[:, sl], ALU.mult),
                    reads=["kdk", "e2"], writes=[kkey]))
            E(lambda h8=h8, pkey=pkey: P.dma(
                "SP", "qload",
                F_cl[:, 0:TW].rearrange("p (t n) -> p t n", n=128),
                d_qscr.rearrange("(h t p) n -> h p t n", h=H8, p=128)
                [h8, :, :, :],
                ["d_qscr", "e1", pkey], ["qhat", "cuml"], k=2))
            for g in range(max(1, TW // 512)):
                gsl = min(512, TW)
                sl = slice(g * gsl, (g + 1) * gsl)
                E(lambda sl=sl, qh_a=qh_a, qkey=qkey: P.emit(
                    "POOL", lambda q: q.tensor_tensor(
                        qh_a[:, sl], F_cl[:, sl],
                        F_e1[:, PFX + sl.start:PFX + sl.stop], ALU.mult),
                    reads=["qhat", "e1"], writes=[qkey]))

        build_prep(0)
        while prep_q:
            prep_q.pop(0)()
        for h8 in range(NHD):
            hpar = h8 % 2
            KH, QH = A_khP[hpar], A_qhP[hpar]
            kkey, qkey = f"khb{hpar}", f"qhb{hpar}"
            rkkey, pkey = f"rk{hpar}", f"pcf{hpar}"
            rks = sct[:, 24 + hpar * NTA:24 + (hpar + 1) * NTA]
            pcf_a = pcf[:, 0:NCA] if hpar == 0 else nrm[:, 8:8 + NCA]
            P.emit("DVE", lambda q: q.memset(ssb[:, 0:128], 0.0),
                   reads=[], writes=["S0"])

            def _sc(ci):
                sl = slice(ci * C, (ci + 1) * C)
                par = ci % 2
                tile = ci // 2
                row0 = par * 64
                return dict(
                    sl=sl, q_sl=slice(sl.start - PFX, sl.stop - PFX), par=par,
                    tile=tile, row0=row0, cur=par, nxt=(ci + 1) % 2,
                    psK=pt6 if par == 0 else pt7,
                    psO=ps[2 + par], psS=ps[4 + par],
                    kK=f"pt{6 + par}", kO=f"ps{2 + par}", kS=f"ps{4 + par}",
                    v_c=vt[row0:row0 + 64,
                           tile * 1024 + h8 * 128:tile * 1024 + (h8 + 1) * 128],
                    rk_c=rks[row0:row0 + 64, tile:tile + 1],
                    tp=(0, row0) if row0 else None)

            def emit_AB(ci):
                s = _sc(ci)
                par, row0, rk_c = s["par"], s["row0"], s["rk_c"]
                P.emit("PE", lambda q, sl=s["sl"], psK=s["psK"], row0=row0,
                       tp=s["tp"]: q.transpose(
                    psK[row0:row0 + 64, 0:128], KH[:, sl], c_iden[:],
                    tile_position=tp),
                    reads=[kkey, "c_iden"], writes=[s["kK"]])
                P.emit("ACT", lambda q, psK=s["psK"], row0=row0, rk_c=rk_c:
                       q.activation(
                           khts[row0:row0 + 64, 0:128],
                           psK[row0:row0 + 64, 0:128],
                           AF.Copy, bias=0.0, scale=rk_c),
                       reads=[s["kK"], rkkey], writes=[f"khts{par}"])
                if ci >= NCP:
                    P.emit("PE", lambda q, sl=s["sl"], q_sl=s["q_sl"],
                           psS=s["psS"], row0=row0, tp=s["tp"]: q.matmul(
                        psS[row0:row0 + 64, 0:64], KH[:, sl], QH[:, q_sl],
                        start=True, stop=True, tile_position=tp),
                        reads=[kkey, qkey], writes=[s["kS"]])
                    P.emit("DVE", lambda q, psS=s["psS"], row0=row0, rk_c=rk_c:
                           q.scalar_tensor_tensor(
                               amsk[row0:row0 + 64, 0:64],
                               psS[row0:row0 + 64, 0:64],
                               rk_c, c_tri[row0:row0 + 64, :],
                               ALU.mult, ALU.mult),
                           reads=[s["kS"], rkkey, "c_tri"],
                           writes=[f"amsk{par}"])

            def emit_C(ci):
                s = _sc(ci)
                par, row0 = s["par"], s["row0"]
                cur, nxt = s["cur"], s["nxt"]
                psO, kO = s["psO"], s["kO"]
                if ci >= NCP:
                    P.emit("ACT", lambda q, cur=cur: q.activation(
                        sbf[:], ssb[:, cur * 128:cur * 128 + 128],
                        AF.Copy, bias=0.0, scale=1.0),
                        reads=[f"S{cur}"], writes=["Sbf"])
                    P.group("PE", [
                        lambda q, psO=psO, row0=row0, v_c=s["v_c"]: q.matmul(
                            psO[:, 0:64], v_c, amsk[row0:row0 + 64, 0:64],
                            start=True, stop=False),
                        lambda q, psO=psO, q_sl=s["q_sl"]: q.matmul(
                            psO[:, 0:64], sbf[:],
                            QH[:, q_sl], start=False, stop=True),
                        lambda q, psO=psO, row0=row0, v_c=s["v_c"]: q.matmul(
                            psO[:, 64:192], khts[row0:row0 + 64, 0:128],
                            v_c, start=True, stop=True),
                    ], reads=[f"amsk{par}", "vt", "Sbf", qkey, f"khts{par}"],
                        writes=[kO])
                    wc = ci - NCP
                    osl = wc % 2
                    P.emit("ACT", lambda q, psO=psO, osl=osl: q.activation(
                        ostg[:, osl * 64:osl * 64 + 64], psO[:, 0:64],
                        AF.Copy, bias=0.0, scale=1.0),
                        reads=[kO], writes=[f"ostg{osl}"])
                    if osl == 1 or ci == NCA - 1:
                        tloc = (wc - osl) * C
                        P.dma("ACT", "ostore",
                              d_o[h8 * 128:(h8 + 1) * 128,
                                  tloc:tloc + (osl + 1) * C],
                              ostg[:, 0:(osl + 1) * 64],
                              [f"ostg{k}" for k in range(osl + 1)],
                              ["d_o"], k=4)
                else:
                    P.emit("PE", lambda q, psO=psO, row0=row0, v_c=s["v_c"]:
                           q.matmul(
                               psO[:, 64:192], khts[row0:row0 + 64, 0:128],
                               v_c, start=True, stop=True),
                           reads=["vt", f"khts{par}"], writes=[kO])
                P.emit("DVE", lambda q, psO=psO, cur=cur, nxt=nxt:
                       q.scalar_tensor_tensor(
                           ssb[:, nxt * 128:nxt * 128 + 128],
                           psO[:, 64:192], 1.0,
                           ssb[:, cur * 128:cur * 128 + 128],
                           ALU.bypass, ALU.add),
                       reads=[f"S{cur}", kO] +
                             ([f"ostg{(ci - NCP) % 2}"] if ci >= NCP else []),
                       writes=[f"S{nxt}"])
                P.emit("DVE", lambda q, ci=ci, nxt=nxt, pcf_a=pcf_a:
                       q.tensor_scalar(
                           ssb[:, nxt * 128:nxt * 128 + 128],
                           ssb[:, nxt * 128:nxt * 128 + 128],
                           pcf_a[:, ci:ci + 1], None, ALU.mult),
                       reads=[f"S{nxt}", pkey], writes=[f"S{nxt}"])

            if h8 + 1 < NHD:
                build_prep(h8 + 1)
            NCR = NCA if _P2SUB > 5 else min(NCA, _NCHDBG)
            emit_AB(0)
            for ci in range(NCR):
                if ci + 1 < NCR:
                    emit_AB(ci + 1)
                emit_C(ci)
                for _ in range(3):
                    if prep_q:
                        prep_q.pop(0)()
            while prep_q:
                prep_q.pop(0)()
            _emit_mean_chunk()
            _emit_mean_chunk()
        while mean_jobs:
            _emit_mean_chunk()

        P.emit("DVE", lambda q: q.tensor_reduce(
            wmp[:, 16:17], wmp[:, 0:8], AX.X, ALU.add), reads=["wmp"], writes=["wmp"])
        P.emit("DVE", lambda q: q.tensor_reduce(
            wmp[:, 17:18], wmp[:, 8:16], AX.X, ALU.add), reads=["wmp"], writes=["wmp"])
        for wi in range(2):
            P.emit("PE", lambda q, wi=wi: q.transpose(
                ps[4][0:1, wi * 128:(wi + 1) * 128], wmp[:, 16 + wi:17 + wi],
                c_idenf[:]), reads=["wmp", "c_idenf"], writes=["ps4"])
        P.emit("ACT", lambda q: q.activation(
            u1b[0:1, 0:256], ps[4][0:1, 0:256], AF.Copy, bias=0.0, scale=1.0),
            reads=["ps4"], writes=["u1b"])
        for wi in range(2):
            P.emit("DVE", lambda q, wi=wi: q.tensor_reduce(
                wrow[0:1, wi:wi + 1], u1b[0:1, wi * 128:(wi + 1) * 128],
                AX.X, ALU.add), reads=["u1b"], writes=["wrow"])
        P.emit("DVE", lambda q: q.tensor_scalar(
            wrow[0:1, 2:4], wrow[0:1, 0:2], 1.0 / (4096.0 * 1024.0), 1e-5,
            ALU.mult, ALU.max), reads=["wrow"], writes=["wrow"])
        P.emit("DVE", lambda q: q.tensor_scalar(
            wrow[0:1, 4:6], wrow[0:1, 2:4], 1.0 / 127.0, None, ALU.mult),
            reads=["wrow"], writes=["wrow"])
        for wi in range(2):
            P.emit("DVE", lambda q, wi=wi: q.tensor_scalar(
                wrow[0:1, 6 + 2 * wi:7 + 2 * wi], wrow[0:1, 2 + wi:3 + wi],
                -0.5, None, ALU.mult), reads=["wrow"], writes=["wrow"])
            P.emit("DVE", lambda q, wi=wi: q.tensor_scalar(
                wrow[0:1, 7 + 2 * wi:8 + 2 * wi], wrow[0:1, 2 + wi:3 + wi],
                0.5, None, ALU.mult), reads=["wrow"], writes=["wrow"])
        P.emit("DVE", lambda q: q.memset(lneg[0:1, 0:128], 1.0),
               reads=[], writes=["lneg"])
        P.emit("PE", lambda q: q.matmul(
            ps[4][:, 8:12], lneg[0:1, 0:128], wrow[0:1, 2:6], start=True,
            stop=True,
            skip_group_check=True), reads=["wrow", "lneg"], writes=["ps4"])
        P.emit("PE", lambda q: q.matmul(
            ps[4][:, 12:16], lneg[0:1, 0:128], wrow[0:1, 6:10], start=True,
            stop=True,
            skip_group_check=True), reads=["wrow", "lneg"], writes=["ps4"])
        P.emit("DVE", lambda q: q.tensor_copy(scp[:, 0:4], ps[4][:, 8:12]),
               reads=["ps4"], writes=["scp"])
        P.emit("DVE", lambda q: q.tensor_copy(scp[:, 20:24], ps[4][:, 12:16]),
               reads=["ps4"], writes=["scp"])

        P.barrier()
        tern_jobs = []
        for wi_, dsrc_ in enumerate((dw1t, dw2t)):
            for cch_ in range(8):
                for ss_ in range(8):
                    tern_jobs.append((wi_, cch_, ss_, dsrc_))
        WST = [lneg[:, 3072:3584], lneg[:, 3584:4096]]
        TMP = [wsl[:, 15360:15872], wsl[:, 15872:16384]]

        def _emit_tern():
            if not tern_jobs:
                return
            wi, cch, ss, dsrc = tern_jobs.pop(0)
            idx = wi * 64 + cch * 8 + ss
            stg = WST[idx % 2]
            tmp = TMP[idx % 2]
            skey = f"wst{idx % 2}"
            mkey = f"wsg{idx % 2}"
            if wi == 0:
                wsrc = dsrc.rearrange("(c p) n -> p c n", p=128) \
                    [:, cch, ss * 512:(ss + 1) * 512]
            else:
                wsrc = dsrc.rearrange("(c p) n -> p c n", p=128) \
                    [:, cch * 4 + ss // 2, (ss % 2) * 512:(ss % 2) * 512 + 512]
            alias = ["lneg"] if idx == 0 else []
            P.dma("POOL", "wquant", stg, wsrc, [f"d_w{wi + 1}"], [skey] + alias)
            P.emit("DVE", lambda q, stg=stg, tmp=tmp, wi=wi: q.tensor_scalar(
                tmp, stg, scp[:, 20 + 2 * wi:21 + 2 * wi], None, ALU.is_lt),
                reads=[skey, "scp"], writes=[mkey])
            tq = t1q if wi == 0 else t2q
            tkey = "tq0" if wi == 0 else "tq1"
            P.emit("DVE", lambda q, tq=tq, cch=cch, ss=ss, stg=stg, tmp=tmp,
                   wi=wi: q.scalar_tensor_tensor(
                       tq[:, cch * 4096 + ss * 512:cch * 4096 + ss * 512 + 512],
                       stg, scp[:, 21 + 2 * wi:22 + 2 * wi], tmp,
                       ALU.is_gt, ALU.subtract),
                   reads=[skey, mkey, "scp"], writes=[tkey])

        P.dma("POOL", "wload2", WO.rearrange("p (h n) -> p h n", h=8),
              dwo.rearrange("(h p) n -> p h n", p=128), ["d_wo"], ["wo"])
        P.dma("SP", "bload", cuml[:], db1[:], ["d_b1"], ["cuml"])
        P.dma("POOL", "bload2", b2b[:], db2[:], ["d_b2"], ["b2b"])

        def p4_A(t):
            oslot = t % 2
            otile = OTL[:, oslot * 1024:oslot * 1024 + 1024]
            P.dma("SP", "oload", otile.rearrange("p (h n) -> p h n", h=8),
                  d_o.rearrange("(h p) tt -> p h tt", p=128)
                  [:, :, t * 128:(t + 1) * 128],
                  ["d_o"], [f"otile{oslot}"], k=4)
            mms = []
            for half in range(2):
                for h8 in range(8):
                    mms.append(lambda q, h8=h8, half=half, otile=otile: q.matmul(
                        ps[half][:, 0:512], otile[:, h8 * 128:(h8 + 1) * 128],
                        WO[:, h8 * 1024 + half * 512:h8 * 1024 + half * 512 + 512],
                        start=(h8 == 0), stop=(h8 == 7)))
            P.group("PE", mms, reads=[f"otile{oslot}", "wo"],
                    writes=["ps0", "ps1"])

        p4_A(0)
        for t in range(PT):
            xa = u1b[:]
            P.dma("SP", "xtmload", xa, dxtm[t * 128:(t + 1) * 128, :],
                  ["d_xtm"], ["u1b"], k=4)
            for half in range(2):
                P.emit("DVE", lambda q, half=half, xa=xa: q.scalar_tensor_tensor(
                    remz[:, half * 512:(half + 1) * 512], ps[half][:, 0:512], 1.0,
                    xa[:, half * 512:(half + 1) * 512], ALU.bypass, ALU.add,
                    accum_out=sct[:, 16 + half:17 + half]),
                    reads=[f"ps{half}", "u1b"], writes=["remz", "sct"])
            if t + 1 < PT:
                p4_A(t + 1)
            P.emit("ACT", lambda q: q.activation(
                u1b[:], remz[:], AF.Square, bias=0.0, scale=1.0,
                accum_out=sct[:, 18:19]), reads=["remz"], writes=["u1b", "sct"])
            P.emit("DVE", lambda q: q.scalar_tensor_tensor(
                sct[:, 19:20], sct[:, 16:17], 1.0, sct[:, 17:18],
                ALU.bypass, ALU.add), reads=["sct"], writes=["sct"])
            P.emit("DVE", lambda q: q.tensor_scalar(
                sct[:, 20:21], sct[:, 19:20], 1.0 / D, None, ALU.mult),
                reads=["sct"], writes=["sct"])
            P.emit("DVE", lambda q: q.tensor_tensor(
                sct[:, 21:22], sct[:, 20:21], sct[:, 20:21], ALU.mult),
                reads=["sct"], writes=["sct"])
            P.emit("DVE", lambda q: q.scalar_tensor_tensor(
                sct[:, 22:23], sct[:, 18:19], 1.0 / D, sct[:, 21:22],
                ALU.mult, ALU.subtract), reads=["sct"], writes=["sct"])
            P.emit("ACT", lambda q: q.activation(
                sct[:, 23:24], sct[:, 22:23], AF.Sqrt, bias=cstv[:, 0:1], scale=1.0),
                reads=["sct", "cstv"], writes=["sct"])
            P.emit("DVE", lambda q: q.reciprocal(sct[:, 24:25], sct[:, 23:24]),
                   reads=["sct"], writes=["sct"])
            P.emit("DVE", lambda q: q.tensor_scalar(
                yf32[:], remz[:], sct[:, 20:21], sct[:, 24:25],
                ALU.subtract, ALU.mult), reads=["remz", "sct"], writes=["yf32"])
            P.dma("POOL", "ystore", d_yscr[t * 128:(t + 1) * 128, :], yf32[:],
                  ["yf32"], ["d_yscr"], k=4)
            P.emit("DVE", lambda q: q.tensor_reduce(
                sct[:, 25:26], yf32[:], AX.X, ALU.max, apply_absolute_value=True),
                reads=["yf32"], writes=["sct"])
            P.emit("ACT", lambda q: q.activation(
                u1b[:], yf32[:], AF.Square, bias=0.0, scale=1.0,
                accum_out=sct[:, 26:27]), reads=["yf32"], writes=["u1b", "sct"])
            P.emit("ACT", lambda q: q.activation(
                sct[:, 27:28], sct[:, 26:27], AF.Sqrt, bias=cstv[:, 0:1],
                scale=cstv[:, 1:2]), reads=["sct", "cstv"], writes=["sct"])
            P.emit("DVE", lambda q: q.reciprocal(sct[:, 28:29], sct[:, 27:28]),
                   reads=["sct"], writes=["sct"])
            P.emit("DVE", lambda q: q.tensor_tensor(
                sct[:, 29:30], sct[:, 28:29], sct[:, 25:26], ALU.mult),
                reads=["sct"], writes=["sct"])
            P.emit("DVE", lambda q: q.tensor_scalar(
                sct[:, 30:31], sct[:, 29:30], 1e-5, None, ALU.max),
                reads=["sct"], writes=["sct"])
            P.emit("DVE", lambda q: q.reciprocal(sct[:, 31:32], sct[:, 30:31]),
                   reads=["sct"], writes=["sct"])
            P.emit("DVE", lambda q: q.scalar_tensor_tensor(
                sct[:, 32:33], sct[:, 31:32], 127.0, sct[:, 28:29],
                ALU.mult, ALU.mult), reads=["sct"], writes=["sct"])
            P.emit("DVE", lambda q, t=t: q.scalar_tensor_tensor(
                scp[:, 4 + t:5 + t], sct[:, 30:31], 1.0, scp[:, 2:3],
                ALU.bypass, ALU.mult), reads=["sct"], writes=["scp"])
            P.emit("DVE", lambda q: q.tensor_scalar(
                u1b[:], yf32[:], sct[:, 32:33], None, ALU.mult),
                reads=["yf32", "sct"], writes=["u1b"])
            P.emit("DVE", lambda q: q.tensor_copy(
                yf32[:].bitcast(I32), u1b[:]),
                reads=["u1b"], writes=["yf32"])
            P.emit("DVE", lambda q: q.scalar_tensor_tensor(
                lneg[:, 0:1024], u1b[:], -0.5, yf32[:].bitcast(I32),
                ALU.add, ALU.is_gt),
                reads=["u1b", "yf32"], writes=["lneg"])
            P.emit("DVE", lambda q: q.scalar_tensor_tensor(
                lneg[:, 1024:2048], u1b[:], 0.5, yf32[:].bitcast(I32),
                ALU.add, ALU.is_lt),
                reads=["u1b", "yf32"], writes=["lneg"])
            P.emit("DVE", lambda q: q.scalar_tensor_tensor(
                lneg[:, 2048:3072], yf32[:].bitcast(I32), 1.0, lneg[:, 0:1024],
                ALU.bypass, ALU.add),
                reads=["yf32", "lneg"], writes=["lneg"])
            P.emit("DVE", lambda q: q.tensor_sub(
                N1S[:, 0:1024], lneg[:, 2048:3072], lneg[:, 1024:2048]),
                reads=["lneg"], writes=["n1tm"])
            trs = [(lambda q, cb=cb: q.transpose(
                pt6[:, cb * 128:(cb + 1) * 128], N1S[:, cb * 128:(cb + 1) * 128],
                c_iden[:])) for cb in range(8)]
            P.group("PE", trs, reads=["n1tm", "c_iden"], writes=["pt6"])
            P.emit("ACT", lambda q: q.activation(
                N1S[:, 1024:2048], pt6[:], AF.Copy, bias=0.0, scale=1.0),
                reads=["pt6"], writes=["n1st"])
            P.dma("POOL", "n1store",
                  d_n1scr.rearrange("(c t p) n -> t p c n", c=DC, p=128)
                  [t, :, :, :],
                  N1S[:, 1024:2048].rearrange("p (c n) -> p c n", c=8),
                  ["n1st"], ["d_n1scr"], k=4)
            for _ in range(8):
                _emit_tern()
        while tern_jobs:
            _emit_tern()

        P.barrier()
        WSP = wsl[:, 15360:16384]

        def p5_A(t):
            n1slot = t % 2
            n1a = N1D[:, n1slot * 1024:n1slot * 1024 + 1024]
            ya = YST[:, n1slot * 1024:n1slot * 1024 + 1024]
            P.dma("SP", "n1load", n1a.rearrange("p (c n) -> p c n", c=8),
                  d_n1scr.rearrange("(c t p) n -> t p c n", c=DC, p=128)
                  [t, :, :, :],
                  ["d_n1scr"], [f"n1a{n1slot}"], k=4)
            P.dma("SP", "yload", ya, d_yscr[t * 128:(t + 1) * 128, :],
                  ["d_yscr"], [f"ya{n1slot}"], k=4)
            for s in range(8):
                bank = s % 4
                mms = [(lambda q, c=c, s=s, bank=bank, n1a=n1a: q.matmul(
                    ps[bank][:, 0:512], n1a[:, c * 128:(c + 1) * 128],
                    t1q[:, c * 4096 + s * 512:c * 4096 + (s + 1) * 512],
                    start=(c == 0), stop=(c == DC - 1))) for c in range(DC)]
                P.group("PE", mms, reads=[f"n1a{n1slot}", "tq0"],
                        writes=[f"ps{bank}"])
                P.emit("DVE", lambda q, bank=bank, s=s, t=t:
                       q.scalar_tensor_tensor(
                           lneg[:, s * 512:(s + 1) * 512], ps[bank][:, 0:512],
                           scp[:, 4 + t:5 + t], cuml[:, s * 512:(s + 1) * 512],
                           ALU.mult, ALU.add),
                       reads=[f"ps{bank}", "scp", "cuml"],
                       writes=["lneg", "wst0", "wst1"])
                P.emit("ACT", lambda q, s=s: q.activation(
                    lneg[:, s * 512:(s + 1) * 512],
                    lneg[:, s * 512:(s + 1) * 512],
                    AF.Gelu, bias=0.0, scale=1.0),
                    reads=["lneg"], writes=["lneg"])

        def p5_B(t):
            for qq in range(4):
                qsl = slice(qq * 1024, (qq + 1) * 1024)
                P.emit("DVE", lambda q, qsl=qsl, qq=qq: q.tensor_reduce(
                    sct[:, 36 + qq:37 + qq], lneg[:, qsl], AX.X, ALU.max,
                    apply_absolute_value=True),
                    reads=["lneg"], writes=["sctB"])
                P.emit("ACT", lambda q, qsl=qsl, qq=qq: q.activation(
                    remz[:], lneg[:, qsl], AF.Square, bias=0.0, scale=1.0,
                    accum_out=sct[:, 60 + qq:61 + qq]),
                    reads=["lneg"], writes=["remz", "sctB"])
            P.emit("DVE", lambda q: q.tensor_reduce(
                sct[:, 42:43], sct[:, 36:40], AX.X, ALU.max),
                reads=["sctB"], writes=["sctB"])
            P.emit("DVE", lambda q: q.tensor_reduce(
                sct[:, 45:46], sct[:, 60:64], AX.X, ALU.add),
                reads=["sctB"], writes=["sctB"])
            P.emit("ACT", lambda q: q.activation(
                sct[:, 46:47], sct[:, 45:46], AF.Sqrt, bias=cstv[:, 0:1],
                scale=cstv[:, 2:3]), reads=["sctB", "cstv"], writes=["sctB"])
            P.emit("DVE", lambda q: q.reciprocal(sct[:, 47:48], sct[:, 46:47]),
                   reads=["sctB"], writes=["sctB"])
            P.emit("DVE", lambda q: q.tensor_tensor(
                sct[:, 48:49], sct[:, 47:48], sct[:, 42:43], ALU.mult),
                reads=["sctB"], writes=["sctB"])
            P.emit("DVE", lambda q: q.tensor_scalar(
                sct[:, 49:50], sct[:, 48:49], 1e-5, None, ALU.max),
                reads=["sctB"], writes=["sctB"])
            P.emit("DVE", lambda q: q.reciprocal(sct[:, 50:51], sct[:, 49:50]),
                   reads=["sctB"], writes=["sctB"])
            P.emit("DVE", lambda q: q.scalar_tensor_tensor(
                sct[:, 51:52], sct[:, 50:51], 127.0, sct[:, 47:48],
                ALU.mult, ALU.mult), reads=["sctB"], writes=["sctB"])
            P.emit("DVE", lambda q: q.scalar_tensor_tensor(
                sct[:, 52:53], sct[:, 49:50], 1.0, scp[:, 3:4],
                ALU.bypass, ALU.mult), reads=["sctB", "scp"], writes=["sctB"])
            for qq in range(4):
                qsl = slice(qq * 1024, (qq + 1) * 1024)
                P.emit("DVE", lambda q, qsl=qsl: q.tensor_scalar(
                    u1b[:], lneg[:, qsl], sct[:, 51:52], None, ALU.mult),
                    reads=["lneg", "sctB"], writes=["u1b"])
                P.emit("DVE", lambda q: q.tensor_copy(
                    yf32[:].bitcast(I32), u1b[:]),
                    reads=["u1b"], writes=["yf32"])
                P.emit("DVE", lambda q: q.scalar_tensor_tensor(
                    remz[:], u1b[:], -0.5, yf32[:].bitcast(I32),
                    ALU.add, ALU.is_gt),
                    reads=["u1b", "yf32"], writes=["remz"])
                P.emit("DVE", lambda q: q.scalar_tensor_tensor(
                    WSP, u1b[:], 0.5, yf32[:].bitcast(I32),
                    ALU.add, ALU.is_lt),
                    reads=["u1b", "yf32"], writes=["wspD", "wsg0", "wsg1"])
                P.emit("DVE", lambda q: q.scalar_tensor_tensor(
                    u1b[:], yf32[:].bitcast(I32), 1.0, remz[:],
                    ALU.bypass, ALU.add),
                    reads=["yf32", "remz"], writes=["u1b"])
                P.emit("DVE", lambda q, qsl=qsl: q.tensor_sub(
                    N2TM[:, qsl], u1b[:], WSP),
                    reads=["u1b", "wspD"], writes=["n2tm"])
            for bb in range(4):
                pv = pt6 if bb % 2 == 0 else pt7
                pk = f"pt{6 + bb % 2}"
                trs = [(lambda q, cb=bb * 8 + kk, kk=kk, pv=pv: q.transpose(
                    pv[:, kk * 128:(kk + 1) * 128],
                    N2TM[:, cb * 128:(cb + 1) * 128], c_iden[:]))
                    for kk in range(8)]
                P.group("PE", trs, reads=["n2tm", "c_iden"], writes=[pk])
                P.emit("ACT", lambda q, bb=bb, pv=pv: q.activation(
                    N2F[:, bb * 1024:(bb + 1) * 1024], pv[:],
                    AF.Copy, bias=0.0, scale=1.0),
                    reads=[pk], writes=["n2F"])

        def p5_C(t):
            n1slot = t % 2
            ya = YST[:, n1slot * 1024:n1slot * 1024 + 1024]
            mms = []
            for c2 in range(FC):
                for half in range(2):
                    mms.append(lambda q, c2=c2, half=half: q.matmul(
                        ps[4 + half][:, 0:512], N2F[:, c2 * 128:(c2 + 1) * 128],
                        t2q[:, c2 * 1024 + half * 512:
                            c2 * 1024 + (half + 1) * 512],
                        start=(c2 == 0), stop=(c2 == FC - 1)))
            P.group("PE", mms, reads=["n2F", "tq1"], writes=["ps4", "ps5"])
            for half in range(2):
                P.emit("DVE", lambda q, half=half: q.scalar_tensor_tensor(
                    remz[:, half * 512:(half + 1) * 512],
                    ps[4 + half][:, 0:512],
                    sct[:, 52:53], b2b[:, half * 512:(half + 1) * 512],
                    ALU.mult, ALU.add),
                    reads=[f"ps{4 + half}", "sctB", "b2b"], writes=["remz"])
            P.emit("DVE", lambda q, ya=ya, n1slot=n1slot: q.scalar_tensor_tensor(
                yf32[:], remz[:], 1.0, ya, ALU.bypass, ALU.add,
                accum_out=sct[:, 53:54]),
                reads=["remz", f"ya{n1slot}"], writes=["yf32", "sctC"])
            P.emit("ACT", lambda q: q.activation(
                u1b[:], yf32[:], AF.Square, bias=0.0, scale=1.0,
                accum_out=sct[:, 54:55]), reads=["yf32"],
                writes=["u1b", "sctC"])
            P.emit("DVE", lambda q: q.tensor_scalar(
                sct[:, 55:56], sct[:, 53:54], 1.0 / D, None, ALU.mult),
                reads=["sctC"], writes=["sctC"])
            P.emit("DVE", lambda q: q.tensor_tensor(
                sct[:, 56:57], sct[:, 55:56], sct[:, 55:56], ALU.mult),
                reads=["sctC"], writes=["sctC"])
            P.emit("DVE", lambda q: q.scalar_tensor_tensor(
                sct[:, 57:58], sct[:, 54:55], 1.0 / D, sct[:, 56:57],
                ALU.mult, ALU.subtract), reads=["sctC"], writes=["sctC"])
            P.emit("ACT", lambda q: q.activation(
                sct[:, 58:59], sct[:, 57:58], AF.Sqrt, bias=cstv[:, 0:1],
                scale=1.0),
                reads=["sctC", "cstv"], writes=["sctC"])
            P.emit("DVE", lambda q: q.reciprocal(sct[:, 59:60], sct[:, 58:59]),
                   reads=["sctC"], writes=["sctC"])
            P.emit("DVE", lambda q: q.tensor_scalar(
                remz[:], yf32[:], sct[:, 55:56], sct[:, 59:60],
                ALU.subtract, ALU.mult),
                reads=["yf32", "sctC"], writes=["remz"])
            P.dma("POOL", "outstore", dout[t * 128:(t + 1) * 128, :], remz[:],
                  ["remz"], ["d_out"], k=4)

        p5_A(0)
        for t in range(PT):
            p5_B(t)
            if t + 1 < PT:
                p5_A(t + 1)
            p5_C(t)

        P.barrier()
    return nc


def _consts():
    import ml_dtypes
    bf = ml_dtypes.bfloat16
    return {
        "iden": np.eye(128, dtype=np.float32).astype(bf),
        "idenf": np.eye(128, dtype=np.float32),
        "tri": np.vstack([np.triu(np.ones((64, 64), dtype=np.float32))] * 2).astype(bf),
        "onescol": np.ones((128, 1), dtype=np.float32).astype(bf),
    }


def kernel(L0_attn_x, L0_attn_Wq, L0_attn_Wk, L0_attn_Wv, L0_attn_Wg,
           L0_attn_Wo, L0_ffn_w1, L0_ffn_b1, L0_ffn_w2, L0_ffn_b2):
    x = np.ascontiguousarray(np.asarray(L0_attn_x, dtype=np.float32))
    Wq = np.ascontiguousarray(np.asarray(L0_attn_Wq, dtype=np.float32))
    Wk = np.ascontiguousarray(np.asarray(L0_attn_Wk, dtype=np.float32))
    Wv = np.ascontiguousarray(np.asarray(L0_attn_Wv, dtype=np.float32))
    Wg = np.ascontiguousarray(np.asarray(L0_attn_Wg, dtype=np.float32))
    Wo = np.ascontiguousarray(np.asarray(L0_attn_Wo, dtype=np.float32))
    w1t = np.ascontiguousarray(np.asarray(L0_ffn_w1, dtype=np.float32).T)
    w2t = np.ascontiguousarray(np.asarray(L0_ffn_w2, dtype=np.float32).T)
    b1 = np.ascontiguousarray(
        np.broadcast_to(np.asarray(L0_ffn_b1, dtype=np.float32), (128, 4096)))
    b2 = np.ascontiguousarray(
        np.broadcast_to(np.asarray(L0_ffn_b2, dtype=np.float32), (128, 1024)))
    B, T, D = x.shape
    TW = T // 2
    PFX = 256
    cst = _consts()
    in_maps = []
    for c in range(8):
        b, par = c // 2, c % 2
        xT = x[b].T
        if par == 0:
            xdm = np.hstack([np.zeros((D, PFX), np.float32), xT[:, 0:TW]])
        else:
            xdm = xT[:, TW - PFX:T]
        m = {
            "xdm": np.ascontiguousarray(xdm),
            "xtm": np.ascontiguousarray(x[b, par * TW:(par + 1) * TW]),
            "wq": Wq, "wk": Wk, "wg": Wg, "wv": Wv,
            "wo": Wo, "w1t": w1t, "w2t": w2t, "b1": b1, "b2": b2,
        }
        m.update(cst)
        in_maps.append(m)
    nc = build_program(TW=TW, PFX=PFX)
    res = run_bass_kernel_spmd(nc, in_maps, core_ids=list(range(8)))
    out = np.empty((B, T, D), dtype=np.float32)
    for c in range(8):
        b, par = c // 2, c % 2
        out[b, par * TW:(par + 1) * TW] = np.asarray(res.results[c]["out"])
    return out


# version 24
# speedup vs baseline: 1.0079x; 1.0079x over previous
import numpy as np
from contextlib import ExitStack

import concourse.bass as bass
import concourse.mybir as mybir
from concourse.bass_utils import run_bass_kernel_spmd

F32 = mybir.dt.float32
I32 = mybir.dt.int32
BF16 = mybir.dt.bfloat16
AF = mybir.ActivationFunctionType
ALU = mybir.AluOpType
AX = mybir.AxisListType

ENGS = ("PE", "DVE", "ACT", "POOL", "SP")
_P2SUB = 99
_NHEADS = 99
_NCHDBG = 4


class Prog:

    def __init__(self, nc, ctx):
        self.nc = nc
        self.ctx = ctx
        self.eng = {"PE": nc.tensor, "DVE": nc.vector, "ACT": nc.scalar,
                    "POOL": nc.gpsimd, "SP": nc.sync}
        self.engsem = {}
        self.pos = {}
        for e in ("PE", "DVE", "ACT", "POOL"):
            self.engsem[e] = ctx.enter_context(nc.semaphore("qs_" + e))
            self.pos[e] = 0
        self.streams = {}
        self.satisfied = {e: {} for e in ENGS}
        self.res = {}

    def stream(self, name, k=2):
        if name not in self.streams:
            sems = []
            for i in range(k):
                h = self.ctx.enter_context(self.nc.semaphore(f"dm_{name}{i}"))
                sems.append([f"dm_{name}{i}", h, 0])
            self.streams[name] = {"sems": sems, "rr": 0}
        return self.streams[name]

    def _r(self, key):
        if key not in self.res:
            self.res[key] = {"w": None, "r": []}
        return self.res[key]

    def _wait(self, ename, tokens):
        best = {}
        for (sname, h, v) in tokens:
            cur = best.get(sname)
            if cur is None or v > cur[1]:
                best[sname] = (h, v)
        q = self.eng[ename]
        for sname in sorted(best):
            h, v = best[sname]
            if ename == "PE" and sname == "qs_PE":
                continue
            if v <= self.satisfied[ename].get(sname, -1):
                continue
            q.wait_ge(h, v)
            self.satisfied[ename][sname] = v

    def _deps(self, reads, writes):
        toks = []
        for k in reads:
            r = self._r(k)
            if r["w"] is not None:
                toks.append(r["w"])
        for k in writes:
            r = self._r(k)
            if r["w"] is not None:
                toks.append(r["w"])
            toks.extend(r["r"])
        return toks

    def _commit(self, tok, reads, writes):
        for k in writes:
            r = self._r(k)
            r["w"] = tok
            r["r"] = []
        for k in reads:
            if k not in writes:
                self._r(k)["r"].append(tok)

    def emit(self, ename, fn, reads=(), writes=()):
        self._wait(ename, self._deps(reads, writes))
        instr = fn(self.eng[ename])
        self.pos[ename] += 1
        instr.then_inc(self.engsem[ename], 1)
        self._commit(("qs_" + ename, self.engsem[ename], self.pos[ename]),
                     reads, writes)

    def group(self, ename, fns, reads=(), writes=()):
        self._wait(ename, self._deps(reads, writes))
        instr = None
        for fn in fns:
            instr = fn(self.eng[ename])
        self.pos[ename] += 1
        instr.then_inc(self.engsem[ename], 1)
        self._commit(("qs_" + ename, self.engsem[ename], self.pos[ename]),
                     reads, writes)

    def dma(self, ename, stream_name, out, in_, reads=(), writes=(), k=2, **kw):
        st = self.stream(stream_name, k)
        slot = st["sems"][st["rr"] % len(st["sems"])]
        st["rr"] += 1
        toks = self._deps(reads, writes)
        if slot[2] > 0:
            toks.append((slot[0], slot[1], slot[2]))
        self._wait(ename, toks)
        instr = self.eng[ename].dma_start(out=out, in_=in_, **kw)
        slot[2] += 16
        instr.then_inc(slot[1], 16)
        self._commit((slot[0], slot[1], slot[2]), reads, writes)

    def barrier(self):
        marks = [("qs_" + e, self.engsem[e], self.pos[e])
                 for e in ("PE", "DVE", "ACT", "POOL")]
        for st in self.streams.values():
            for sname, h, tot in st["sems"]:
                if tot > 0:
                    marks.append((sname, h, tot))
        for e in ENGS:
            self._wait(e, marks)


def build_program(TW=2048, PFX=512):
    D = 1024
    H8 = 8
    C = 64
    FF = 4096
    DC = D // 128
    TA = TW + PFX
    NCA = TA // C
    NCP = PFX // C
    NTA = TA // 128
    NTP = PFX // 128
    PT = TW // 128
    FC = FF // 128

    nc = bass.Bass("TRN2", target_bir_lowering=False)

    dxdm = nc.dram_tensor("xdm", [D, TA], F32, kind="ExternalInput").ap()
    dxtm = nc.dram_tensor("xtm", [TW, D], F32, kind="ExternalInput").ap()
    dwq = nc.dram_tensor("wq", [D, H8 * 128], F32, kind="ExternalInput").ap()
    dwk = nc.dram_tensor("wk", [D, H8 * 128], F32, kind="ExternalInput").ap()
    dwg = nc.dram_tensor("wg", [D, H8 * 128], F32, kind="ExternalInput").ap()
    dwv = nc.dram_tensor("wv", [D, H8 * 128], F32, kind="ExternalInput").ap()
    dwo = nc.dram_tensor("wo", [H8 * 128, D], F32, kind="ExternalInput").ap()
    dw1t = nc.dram_tensor("w1t", [D, FF], F32, kind="ExternalInput").ap()
    dw2t = nc.dram_tensor("w2t", [FF, D], F32, kind="ExternalInput").ap()
    db1 = nc.dram_tensor("b1", [128, FF], F32, kind="ExternalInput").ap()
    db2 = nc.dram_tensor("b2", [128, D], F32, kind="ExternalInput").ap()
    diden = nc.dram_tensor("iden", [128, 128], BF16, kind="ExternalInput").ap()
    dtri = nc.dram_tensor("tri", [128, C], BF16, kind="ExternalInput").ap()
    dones = nc.dram_tensor("onescol", [128, 1], BF16, kind="ExternalInput").ap()
    didenf = nc.dram_tensor("idenf", [128, 128], F32, kind="ExternalInput").ap()
    dout = nc.dram_tensor("out", [TW, D], F32, kind="ExternalOutput").ap()

    d_o = nc.dram_tensor("od", [H8 * 128, TW], BF16, kind="Internal").ap()
    d_yscr = nc.dram_tensor("yscr", [TW, D], BF16, kind="Internal").ap()
    d_n1scr = nc.dram_tensor("n1scr", [DC * PT * 128, 128], BF16, kind="Internal").ap()
    d_qscr = nc.dram_tensor("qscr", [H8 * PT * 128, 128], F32, kind="Internal").ap()

    with ExitStack() as ctx:
        e = ctx.enter_context
        biga = e(nc.sbuf_tensor("biga", [128, 32768], BF16))
        bigb = e(nc.sbuf_tensor("bigb", [128, 32768], BF16))
        wsl = e(nc.sbuf_tensor("wsl", [128, 16384], BF16))
        lneg = e(nc.sbuf_tensor("lneg", [128, 4096], F32))
        cuml = e(nc.sbuf_tensor("cuml", [128, 4096], F32))
        remz = e(nc.sbuf_tensor("remz", [128, 1024], F32))
        yf32 = e(nc.sbuf_tensor("yf32", [128, 1024], F32))
        u1b = e(nc.sbuf_tensor("u1b", [128, 1024], F32))
        pcf = e(nc.sbuf_tensor("pcf", [128, NCA], F32))
        nrm = e(nc.sbuf_tensor("nrm", [128, 8 + 3 * NTA], F32))
        sct = e(nc.sbuf_tensor("sct", [128, 64], F32))
        scp = e(nc.sbuf_tensor("scp", [128, 24], F32))
        wmp = e(nc.sbuf_tensor("wmp", [128, 18], F32))
        wrow = e(nc.sbuf_tensor("wrow", [1, 12], F32))
        khts = e(nc.sbuf_tensor("khts", [128, 128], BF16))
        ssb = e(nc.sbuf_tensor("ssb", [128, 256], F32))
        sbf = e(nc.sbuf_tensor("sbf", [128, 128], BF16))
        ostg = e(nc.sbuf_tensor("ostg", [128, 128], BF16))
        amsk = e(nc.sbuf_tensor("amsk", [128, 64], BF16))
        c_iden = e(nc.sbuf_tensor("c_iden", [128, 128], BF16))
        c_idenf = e(nc.sbuf_tensor("c_idenf", [128, 128], F32))
        c_tri = e(nc.sbuf_tensor("c_tri", [128, C], BF16))
        c_ones = e(nc.sbuf_tensor("c_ones", [128, 1], BF16))
        cstv = e(nc.sbuf_tensor("cstv", [128, 3], F32))

        ps = [e(nc.psum_tensor(f"ps{i}", [128, 512], F32)) for i in range(6)]
        pt6 = e(nc.psum_tensor("pt6", [128, 1024], BF16))
        pt7 = e(nc.psum_tensor("pt7", [128, 1024], BF16))

        block = e(nc.Block())
        P = Prog(nc, ctx)

        vt = biga[:, 0:NTA * 1024]
        qstg = biga[:, NTA * 1024:NTA * 1024 + 2048].bitcast(F32)
        XB0 = NTA * 1024 + 2048
        assert XB0 + 5 * TA <= 32768 and 8192 + 3 * TA <= 16384
        xres = []
        for c in range(8):
            if c >= 3:
                xres.append(biga[:, XB0 + (c - 3) * TA:XB0 + (c - 2) * TA])
            else:
                xres.append(wsl[:, 8192 + c * TA:8192 + (c + 1) * TA])
        XR = [f"xr{c}" for c in range(8)]
        t2q = biga
        F_k = bigb[:, 0:2 * TA].bitcast(F32)
        F_lng = bigb[:, 2 * TA:2 * TA + 1024].bitcast(F32)
        B1 = 2 * TA + 1024
        F_cl = bigb[:, B1:B1 + 2 * TA].bitcast(F32)
        F_e1 = bigb[:, B1 + 2 * TA:B1 + 4 * TA].bitcast(F32)
        F_e2 = bigb[:, B1 + 4 * TA:B1 + 6 * TA].bitcast(F32)
        B0 = B1 + 6 * TA
        A_ksq = bigb[:, B0:B0 + TA]
        A_khP = [bigb[:, B0 + TA:B0 + 2 * TA],
                 bigb[:, B0 + 2 * TA:B0 + 3 * TA]]
        A_qhP = [bigb[:, B0 + 3 * TA:B0 + 3 * TA + TW],
                 bigb[:, B0 + 3 * TA + TW:B0 + 3 * TA + 2 * TW]]
        assert B0 + 3 * TA + 2 * TW <= 32768
        t1q = bigb
        WQf = wsl[:, 0:8192]
        WVf = wsl[:, 0:8192]
        WKh = wsl[:, 0:1024]
        WGh = wsl[:, 1024:2048]
        WO = wsl[:, 0:8192]
        OTL = wsl[:, 8192:12288]
        N1S = wsl[:, 12288:14336]
        b2b = wsl[:, 14336:15360]
        N2TM = wsl[:, 0:4096]
        N2F = wsl[:, 4096:8192]
        N1D = wsl[:, 8192:10240]
        YST = wsl[:, 10240:12288]

        P.emit("DVE", lambda q: q.memset(cstv[:, 0:1], 1e-5), writes=["cstv"])
        P.emit("DVE", lambda q: q.memset(cstv[:, 1:2], 1.0 / D), writes=["cstv"])
        P.emit("DVE", lambda q: q.memset(cstv[:, 2:3], 1.0 / FF), writes=["cstv"])
        P.dma("SP", "consts", c_iden[:], diden[:], ["d_iden"], ["c_iden"], k=4)
        P.dma("SP", "consts", c_idenf[:], didenf[:], ["d_idenf"], ["c_idenf"], k=4)
        P.dma("SP", "consts", c_tri[:], dtri[:], ["d_tri"], ["c_tri"], k=4)
        P.dma("SP", "consts", c_ones[:], dones[:], ["d_ones"], ["c_ones"], k=4)
        P.dma("POOL", "wload", WVf.rearrange("p (c n) -> p c n", n=1024),
              dwv.rearrange("(c p) n -> p c n", p=128),
              ["d_wv"], ["sl_wv", "sl_wq"], k=4)
        for c in range(8):
            P.dma("POOL", "xload", xres[c][:],
                  dxdm.rearrange("(c p) tt -> p c tt", p=128)[:, c, :],
                  ["d_xdm"], [f"xr{c}"], k=4)

        for t in range(NTA):
            for half in range(2):
                mv = [(lambda q, c=c, t=t, half=half: q.matmul(
                    ps[half][:, 0:512], xres[c][:, t * 128:(t + 1) * 128],
                    WVf[:, c * 1024 + half * 512:c * 1024 + half * 512 + 512],
                    start=(c == 0), stop=(c == DC - 1))) for c in range(DC)]
                P.group("PE", mv, reads=XR + ["sl_wv"], writes=[f"ps{half}"])
                P.emit("ACT", lambda q, t=t, half=half: q.activation(
                    vt[:, t * 1024 + half * 512:t * 1024 + half * 512 + 512],
                    ps[half][:, 0:512], AF.Copy, bias=0.0, scale=1.0),
                    reads=[f"ps{half}"], writes=["vt"])
        P.dma("POOL", "wload", WQf.rearrange("p (c n) -> p c n", n=1024),
              dwq.rearrange("(c p) n -> p c n", p=128),
              ["d_wq"], ["sl_wq", "sl_wv"], k=4)
        def q_proj(t):
            qslot = t % 2
            qa = lneg[:, qslot * 1024:qslot * 1024 + 1024]
            for half in range(2):
                mq = [(lambda q, c=c, t=t, half=half: q.matmul(
                    ps[2 + half][:, 0:512], xres[c][:, t * 128:(t + 1) * 128],
                    WQf[:, c * 1024 + half * 512:c * 1024 + half * 512 + 512],
                    start=(c == 0), stop=(c == DC - 1))) for c in range(DC)]
                P.group("PE", mq, reads=XR + ["sl_wq"], writes=[f"ps{2 + half}"])
                P.emit("ACT", lambda q, qa=qa, half=half: q.activation(
                    qa[:, half * 512:half * 512 + 512], ps[2 + half][:, 0:512],
                    AF.Copy, bias=0.0, scale=1.0),
                    reads=[f"ps{2 + half}"], writes=[f"qtm{qslot}"])

        def q_tail(t):
            tw = t - NTP
            qslot = t % 2
            qa = lneg[:, qslot * 1024:qslot * 1024 + 1024]
            sc0 = qslot * 24
            nr0 = qslot * 8
            sk, nk = f"sct{qslot}", f"nrmq{qslot}"
            for h8 in range(H8):
                P.emit("DVE", lambda q, qa=qa, h8=h8: q.scalar_tensor_tensor(
                    u1b[:, 0:128], qa[:, h8 * 128:(h8 + 1) * 128], 1.0,
                    qa[:, h8 * 128:(h8 + 1) * 128], ALU.bypass, ALU.mult,
                    accum_out=nrm[:, nr0 + h8:nr0 + h8 + 1]),
                    reads=[f"qtm{qslot}"], writes=["u1b", nk])
            P.emit("ACT", lambda q: q.activation(
                sct[:, sc0:sc0 + H8], nrm[:, nr0:nr0 + H8], AF.Sqrt,
                bias=0.0, scale=1.0), reads=[nk], writes=[sk])
            P.emit("DVE", lambda q: q.tensor_scalar(
                sct[:, sc0 + 8:sc0 + 8 + H8], sct[:, sc0:sc0 + H8],
                1e-12, None, ALU.max), reads=[sk], writes=[sk])
            P.emit("DVE", lambda q: q.reciprocal(
                sct[:, sc0 + 16:sc0 + 16 + H8], sct[:, sc0 + 8:sc0 + 8 + H8]),
                reads=[sk], writes=[sk])
            qb = cuml[:, qslot * 1024:qslot * 1024 + 1024]
            for bb in range(2):
                pv = ps[bb]
                for k4 in range(4):
                    h8 = bb * 4 + k4
                    qh_t = qb[:, h8 * 128:(h8 + 1) * 128]
                    P.emit("DVE", lambda q, qa=qa, h8=h8, qh_t=qh_t, sc0=sc0:
                           q.tensor_scalar(
                               qh_t, qa[:, h8 * 128:(h8 + 1) * 128],
                               sct[:, sc0 + 16 + h8:sc0 + 17 + h8], None,
                               ALU.mult),
                           reads=[f"qtm{qslot}", sk],
                           writes=[f"qhatb{qslot}"])
                    P.emit("PE", lambda q, qh_t=qh_t, k4=k4, pv=pv: q.transpose(
                        pv[:, k4 * 128:(k4 + 1) * 128], qh_t, c_idenf[:]),
                        reads=[f"qhatb{qslot}", "c_idenf"], writes=[f"ps{bb}"])
                qsg = qstg[:, bb * 512:bb * 512 + 512]
                P.emit("DVE", lambda q, pv=pv, qsg=qsg: q.tensor_copy(
                    qsg, pv[:, 0:512]),
                    reads=[f"ps{bb}"], writes=[f"qstg{bb}"])
                P.dma("SP", "qstore",
                      d_qscr.rearrange("(h t p) n -> t p h n", h=H8, p=128)
                      [tw, :, bb * 4:bb * 4 + 4, :],
                      qsg.rearrange("p (h n) -> p h n", h=4),
                      [f"qstg{bb}"], ["d_qscr"], k=4)

        q_proj(NTP)
        for t in range(NTP, NTA):
            if t + 1 < NTA:
                q_proj(t + 1)
            q_tail(t)

        mean_jobs = []
        for wi_, dsrc_ in enumerate((dw1t, dw2t)):
            for cch_ in range(8):
                mean_jobs.append((wi_, cch_, dsrc_))

        def _emit_mean_chunk():
            if not mean_jobs:
                return
            wi, cch, dsrc = mean_jobs.pop(0)
            slb = (wi * 8 + cch) % 2
            buf = lneg if slb == 0 else cuml
            bkey = "lneg" if slb == 0 else "cuml"
            alias = ["qtm0", "qtm1"] if slb == 0 else ["qhatb"]
            if wi == 0:
                wsrc = dsrc.rearrange("(c p) n -> p c n", p=128)[:, cch, :]
                dst = buf[:]
            else:
                wsrc = dsrc.rearrange("(c p) n -> p c n", p=128)                     [:, cch * 4:cch * 4 + 4, :]
                dst = buf[:].rearrange("p (c n) -> p c n", c=4)
            P.dma("SP", "wmean", dst, wsrc, [f"d_w{wi + 1}"], [bkey] + alias)
            P.emit("DVE", lambda q, buf=buf, wi=wi, cch=cch: q.tensor_reduce(
                wmp[:, wi * 8 + cch:wi * 8 + cch + 1], buf[:], AX.X, ALU.add,
                apply_absolute_value=True),
                reads=[bkey], writes=["wmp"])

        prep_q = []
        NHD = min(H8, _NHEADS)

        def build_prep(h8):
            par = h8 % 2
            kh_a, qh_a = A_khP[par], A_qhP[par]
            kkey, qkey = f"khb{par}", f"qhb{par}"
            rks = sct[:, 24 + par * NTA:24 + (par + 1) * NTA]
            pcf_a = pcf[:, 0:NCA] if par == 0 else nrm[:, 8:8 + NCA]
            pkey = f"pcf{par}"
            E = prep_q.append
            E(lambda h8=h8: P.dma(
                "POOL", "wkh", WKh.rearrange("p (c n) -> p c n", n=128),
                dwk.rearrange("(c p) n -> p c n", p=128)
                [:, :, h8 * 128:(h8 + 1) * 128],
                ["d_wk"], ["wkh", "sl_wq"], k=2))
            E(lambda h8=h8: P.dma(
                "POOL", "wgh", WGh.rearrange("p (c n) -> p c n", n=128),
                dwg.rearrange("(c p) n -> p c n", p=128)
                [:, :, h8 * 128:(h8 + 1) * 128],
                ["d_wg"], ["wgh", "sl_wq"], k=2))
            for s0 in range(0, TA, 512):
                gw = min(512, TA - s0)
                sl = slice(s0, s0 + gw)
                mk = [(lambda q, c=c, s0=s0, gw=gw: q.matmul(
                    ps[0][:, 0:gw], WKh[:, c * 128:(c + 1) * 128],
                    xres[c][:, s0:s0 + gw],
                    start=(c == 0), stop=(c == DC - 1))) for c in range(DC)]
                mg = [(lambda q, c=c, s0=s0, gw=gw: q.matmul(
                    ps[1][:, 0:gw], WGh[:, c * 128:(c + 1) * 128],
                    xres[c][:, s0:s0 + gw],
                    start=(c == 0), stop=(c == DC - 1))) for c in range(DC)]
                E(lambda mk=mk: P.group("PE", mk, reads=XR + ["wkh"],
                                        writes=["ps0"]))
                E(lambda sl=sl, gw=gw: P.emit("ACT", lambda q: q.activation(
                    F_k[:, sl], ps[0][:, 0:gw], AF.Copy, bias=0.0, scale=1.0),
                    reads=["ps0"], writes=["kdk"]))
                E(lambda sl=sl, gw=gw: P.emit("ACT", lambda q: q.activation(
                    A_ksq[:, sl], ps[0][:, 0:gw], AF.Square, bias=0.0,
                    scale=1.0), reads=["ps0"], writes=["ksq"]))
                E(lambda mg=mg: P.group("PE", mg, reads=XR + ["wgh"],
                                        writes=["ps1"]))
                E(lambda gw=gw: P.emit("ACT", lambda q: q.activation(
                    remz[:, 0:gw], ps[1][:, 0:gw], AF.Exp, bias=0.0,
                    scale=-1.0), reads=["ps1"], writes=["remz"]))
                E(lambda gw=gw: P.emit("ACT", lambda q: q.activation(
                    F_lng[:, 0:gw], remz[:, 0:gw], AF.Ln, bias=1.0,
                    scale=1.0), reads=["remz"], writes=["lng"]))
                for cc in range(gw // C):
                    ci0 = s0 // C + cc
                    E(lambda cc=cc, ci0=ci0: P.emit(
                        "DVE", lambda q: q.tensor_tensor_scan(
                            F_cl[:, ci0 * C:(ci0 + 1) * C],
                            F_lng[:, cc * C:(cc + 1) * C],
                            F_lng[:, cc * C:(cc + 1) * C],
                            0.0, ALU.add, ALU.bypass),
                        reads=["lng"], writes=["cuml", "qhat"]))
            nmm = [(lambda q, t2=t2: q.matmul(
                ps[0][:, t2:t2 + 1], A_ksq[:, t2 * 128:(t2 + 1) * 128],
                c_ones[:], start=True, stop=True,
                skip_group_check=True)) for t2 in range(NTA)]
            E(lambda nmm=nmm: P.group("PE", nmm, reads=["ksq", "c_ones"],
                                      writes=["ps0"]))
            E(lambda: P.emit("ACT", lambda q: q.activation(
                nrm[:, 8 + NCA:8 + NCA + NTA], ps[0][:, 0:NTA], AF.Sqrt,
                bias=0.0, scale=1.0), reads=["ps0"],
                writes=["nrmk", "nrmq1"]))
            E(lambda: P.emit("DVE", lambda q: q.tensor_scalar(
                nrm[:, 8 + NCA:8 + NCA + NTA], nrm[:, 8 + NCA:8 + NCA + NTA],
                1e-12, None, ALU.max), reads=["nrmk"], writes=["nrmk"]))
            E(lambda rks=rks, par=par: P.emit("DVE", lambda q: q.reciprocal(
                rks, nrm[:, 8 + NCA:8 + NCA + NTA]),
                reads=["nrmk"], writes=[f"rk{par}", "sct0", "sct1"]))
            E(lambda pcf_a=pcf_a, pkey=pkey: P.emit(
                "ACT", lambda q: q.activation(
                    pcf_a, F_cl[:, C - 1:TA:C], AF.Exp, bias=0.0, scale=-1.0),
                reads=["cuml"], writes=[pkey, "nrmq1"]))
            for s0 in range(0, TA, 512):
                sl = slice(s0, min(s0 + 512, TA))
                E(lambda sl=sl: P.emit("ACT", lambda q: q.activation(
                    F_e1[:, sl], F_cl[:, sl], AF.Exp, bias=0.0, scale=-1.0),
                    reads=["cuml"], writes=["e1"]))
                E(lambda sl=sl: P.emit("ACT", lambda q: q.activation(
                    F_e2[:, sl], F_cl[:, sl], AF.Exp, bias=0.0, scale=1.0),
                    reads=["cuml"], writes=["e2"]))
            for s0 in range(0, TA, 512):
                sl = slice(s0, min(s0 + 512, TA))
                E(lambda sl=sl, kh_a=kh_a, kkey=kkey: P.emit(
                    "POOL", lambda q: q.tensor_tensor(
                        kh_a[:, sl], F_k[:, sl], F_e2[:, sl], ALU.mult),
                    reads=["kdk", "e2"], writes=[kkey]))
            E(lambda h8=h8, pkey=pkey: P.dma(
                "ACT", "qload",
                F_cl[:, 0:TW].rearrange("p (t n) -> p t n", n=128),
                d_qscr.rearrange("(h t p) n -> h p t n", h=H8, p=128)
                [h8, :, :, :],
                ["d_qscr", "e1", pkey], ["qhat", "cuml"], k=2))
            for g in range(max(1, TW // 512)):
                gsl = min(512, TW)
                sl = slice(g * gsl, (g + 1) * gsl)
                E(lambda sl=sl, qh_a=qh_a, qkey=qkey: P.emit(
                    "POOL", lambda q: q.tensor_tensor(
                        qh_a[:, sl], F_cl[:, sl],
                        F_e1[:, PFX + sl.start:PFX + sl.stop], ALU.mult),
                    reads=["qhat", "e1"], writes=[qkey]))

        build_prep(0)
        while prep_q:
            prep_q.pop(0)()
        for h8 in range(NHD):
            hpar = h8 % 2
            KH, QH = A_khP[hpar], A_qhP[hpar]
            kkey, qkey = f"khb{hpar}", f"qhb{hpar}"
            rkkey, pkey = f"rk{hpar}", f"pcf{hpar}"
            rks = sct[:, 24 + hpar * NTA:24 + (hpar + 1) * NTA]
            pcf_a = pcf[:, 0:NCA] if hpar == 0 else nrm[:, 8:8 + NCA]
            P.emit("DVE", lambda q: q.memset(ssb[:, 0:128], 0.0),
                   reads=[], writes=["S0"])

            def _sc(ci):
                sl = slice(ci * C, (ci + 1) * C)
                par = ci % 2
                tile = ci // 2
                row0 = par * 64
                return dict(
                    sl=sl, q_sl=slice(sl.start - PFX, sl.stop - PFX), par=par,
                    tile=tile, row0=row0, cur=par, nxt=(ci + 1) % 2,
                    psK=pt6 if par == 0 else pt7,
                    psO=ps[2 + par], psS=ps[4 + par],
                    kK=f"pt{6 + par}", kO=f"ps{2 + par}", kS=f"ps{4 + par}",
                    v_c=vt[row0:row0 + 64,
                           tile * 1024 + h8 * 128:tile * 1024 + (h8 + 1) * 128],
                    rk_c=rks[row0:row0 + 64, tile:tile + 1],
                    tp=(0, row0) if row0 else None)

            def emit_AB(ci):
                s = _sc(ci)
                par, row0, rk_c = s["par"], s["row0"], s["rk_c"]
                P.emit("PE", lambda q, sl=s["sl"], psK=s["psK"], row0=row0,
                       tp=s["tp"]: q.transpose(
                    psK[row0:row0 + 64, 0:128], KH[:, sl], c_iden[:],
                    tile_position=tp),
                    reads=[kkey, "c_iden"], writes=[s["kK"]])
                P.emit("ACT", lambda q, psK=s["psK"], row0=row0, rk_c=rk_c:
                       q.activation(
                           khts[row0:row0 + 64, 0:128],
                           psK[row0:row0 + 64, 0:128],
                           AF.Copy, bias=0.0, scale=rk_c),
                       reads=[s["kK"], rkkey], writes=[f"khts{par}"])
                if ci >= NCP:
                    P.emit("PE", lambda q, sl=s["sl"], q_sl=s["q_sl"],
                           psS=s["psS"], row0=row0, tp=s["tp"]: q.matmul(
                        psS[row0:row0 + 64, 0:64], KH[:, sl], QH[:, q_sl],
                        start=True, stop=True, tile_position=tp),
                        reads=[kkey, qkey], writes=[s["kS"]])
                    P.emit("DVE", lambda q, psS=s["psS"], row0=row0, rk_c=rk_c:
                           q.scalar_tensor_tensor(
                               amsk[row0:row0 + 64, 0:64],
                               psS[row0:row0 + 64, 0:64],
                               rk_c, c_tri[row0:row0 + 64, :],
                               ALU.mult, ALU.mult),
                           reads=[s["kS"], rkkey, "c_tri"],
                           writes=[f"amsk{par}"])

            def emit_C(ci):
                s = _sc(ci)
                par, row0 = s["par"], s["row0"]
                cur, nxt = s["cur"], s["nxt"]
                psO, kO = s["psO"], s["kO"]
                if ci >= NCP:
                    P.emit("ACT", lambda q, cur=cur: q.activation(
                        sbf[:], ssb[:, cur * 128:cur * 128 + 128],
                        AF.Copy, bias=0.0, scale=1.0),
                        reads=[f"S{cur}"], writes=["Sbf"])
                    P.group("PE", [
                        lambda q, psO=psO, row0=row0, v_c=s["v_c"]: q.matmul(
                            psO[:, 0:64], v_c, amsk[row0:row0 + 64, 0:64],
                            start=True, stop=False),
                        lambda q, psO=psO, q_sl=s["q_sl"]: q.matmul(
                            psO[:, 0:64], sbf[:],
                            QH[:, q_sl], start=False, stop=True),
                        lambda q, psO=psO, row0=row0, v_c=s["v_c"]: q.matmul(
                            psO[:, 64:192], khts[row0:row0 + 64, 0:128],
                            v_c, start=True, stop=True),
                    ], reads=[f"amsk{par}", "vt", "Sbf", qkey, f"khts{par}"],
                        writes=[kO])
                    wc = ci - NCP
                    osl = wc % 2
                    P.emit("ACT", lambda q, psO=psO, osl=osl: q.activation(
                        ostg[:, osl * 64:osl * 64 + 64], psO[:, 0:64],
                        AF.Copy, bias=0.0, scale=1.0),
                        reads=[kO], writes=[f"ostg{osl}"])
                    if osl == 1 or ci == NCA - 1:
                        tloc = (wc - osl) * C
                        P.dma("ACT", "ostore",
                              d_o[h8 * 128:(h8 + 1) * 128,
                                  tloc:tloc + (osl + 1) * C],
                              ostg[:, 0:(osl + 1) * 64],
                              [f"ostg{k}" for k in range(osl + 1)],
                              ["d_o"], k=4)
                else:
                    P.emit("PE", lambda q, psO=psO, row0=row0, v_c=s["v_c"]:
                           q.matmul(
                               psO[:, 64:192], khts[row0:row0 + 64, 0:128],
                               v_c, start=True, stop=True),
                           reads=["vt", f"khts{par}"], writes=[kO])
                P.emit("DVE", lambda q, psO=psO, cur=cur, nxt=nxt:
                       q.scalar_tensor_tensor(
                           ssb[:, nxt * 128:nxt * 128 + 128],
                           psO[:, 64:192], 1.0,
                           ssb[:, cur * 128:cur * 128 + 128],
                           ALU.bypass, ALU.add),
                       reads=[f"S{cur}", kO] +
                             ([f"ostg{(ci - NCP) % 2}"] if ci >= NCP else []),
                       writes=[f"S{nxt}"])
                P.emit("DVE", lambda q, ci=ci, nxt=nxt, pcf_a=pcf_a:
                       q.tensor_scalar(
                           ssb[:, nxt * 128:nxt * 128 + 128],
                           ssb[:, nxt * 128:nxt * 128 + 128],
                           pcf_a[:, ci:ci + 1], None, ALU.mult),
                       reads=[f"S{nxt}", pkey], writes=[f"S{nxt}"])

            if h8 + 1 < NHD:
                build_prep(h8 + 1)
            NCR = NCA if _P2SUB > 5 else min(NCA, _NCHDBG)
            emit_AB(0)
            for ci in range(NCR):
                if ci + 1 < NCR:
                    emit_AB(ci + 1)
                emit_C(ci)
                for _ in range(3):
                    if prep_q:
                        prep_q.pop(0)()
            while prep_q:
                prep_q.pop(0)()
            _emit_mean_chunk()
            _emit_mean_chunk()
        while mean_jobs:
            _emit_mean_chunk()

        P.emit("DVE", lambda q: q.tensor_reduce(
            wmp[:, 16:17], wmp[:, 0:8], AX.X, ALU.add), reads=["wmp"], writes=["wmp"])
        P.emit("DVE", lambda q: q.tensor_reduce(
            wmp[:, 17:18], wmp[:, 8:16], AX.X, ALU.add), reads=["wmp"], writes=["wmp"])
        for wi in range(2):
            P.emit("PE", lambda q, wi=wi: q.transpose(
                ps[4][0:1, wi * 128:(wi + 1) * 128], wmp[:, 16 + wi:17 + wi],
                c_idenf[:]), reads=["wmp", "c_idenf"], writes=["ps4"])
        P.emit("ACT", lambda q: q.activation(
            u1b[0:1, 0:256], ps[4][0:1, 0:256], AF.Copy, bias=0.0, scale=1.0),
            reads=["ps4"], writes=["u1b"])
        for wi in range(2):
            P.emit("DVE", lambda q, wi=wi: q.tensor_reduce(
                wrow[0:1, wi:wi + 1], u1b[0:1, wi * 128:(wi + 1) * 128],
                AX.X, ALU.add), reads=["u1b"], writes=["wrow"])
        P.emit("DVE", lambda q: q.tensor_scalar(
            wrow[0:1, 2:4], wrow[0:1, 0:2], 1.0 / (4096.0 * 1024.0), 1e-5,
            ALU.mult, ALU.max), reads=["wrow"], writes=["wrow"])
        P.emit("DVE", lambda q: q.tensor_scalar(
            wrow[0:1, 4:6], wrow[0:1, 2:4], 1.0 / 127.0, None, ALU.mult),
            reads=["wrow"], writes=["wrow"])
        for wi in range(2):
            P.emit("DVE", lambda q, wi=wi: q.tensor_scalar(
                wrow[0:1, 6 + 2 * wi:7 + 2 * wi], wrow[0:1, 2 + wi:3 + wi],
                -0.5, None, ALU.mult), reads=["wrow"], writes=["wrow"])
            P.emit("DVE", lambda q, wi=wi: q.tensor_scalar(
                wrow[0:1, 7 + 2 * wi:8 + 2 * wi], wrow[0:1, 2 + wi:3 + wi],
                0.5, None, ALU.mult), reads=["wrow"], writes=["wrow"])
        P.emit("DVE", lambda q: q.memset(lneg[0:1, 0:128], 1.0),
               reads=[], writes=["lneg"])
        P.emit("PE", lambda q: q.matmul(
            ps[4][:, 8:12], lneg[0:1, 0:128], wrow[0:1, 2:6], start=True,
            stop=True,
            skip_group_check=True), reads=["wrow", "lneg"], writes=["ps4"])
        P.emit("PE", lambda q: q.matmul(
            ps[4][:, 12:16], lneg[0:1, 0:128], wrow[0:1, 6:10], start=True,
            stop=True,
            skip_group_check=True), reads=["wrow", "lneg"], writes=["ps4"])
        P.emit("DVE", lambda q: q.tensor_copy(scp[:, 0:4], ps[4][:, 8:12]),
               reads=["ps4"], writes=["scp"])
        P.emit("DVE", lambda q: q.tensor_copy(scp[:, 20:24], ps[4][:, 12:16]),
               reads=["ps4"], writes=["scp"])

        P.barrier()
        tern_jobs = []
        for wi_, dsrc_ in enumerate((dw1t, dw2t)):
            for cch_ in range(8):
                for ss_ in range(8):
                    tern_jobs.append((wi_, cch_, ss_, dsrc_))
        WST = [lneg[:, 3072:3584], lneg[:, 3584:4096]]
        TMP = [wsl[:, 15360:15872], wsl[:, 15872:16384]]

        def _emit_tern():
            if not tern_jobs:
                return
            wi, cch, ss, dsrc = tern_jobs.pop(0)
            idx = wi * 64 + cch * 8 + ss
            stg = WST[idx % 2]
            tmp = TMP[idx % 2]
            skey = f"wst{idx % 2}"
            mkey = f"wsg{idx % 2}"
            if wi == 0:
                wsrc = dsrc.rearrange("(c p) n -> p c n", p=128) \
                    [:, cch, ss * 512:(ss + 1) * 512]
            else:
                wsrc = dsrc.rearrange("(c p) n -> p c n", p=128) \
                    [:, cch * 4 + ss // 2, (ss % 2) * 512:(ss % 2) * 512 + 512]
            alias = ["lneg"] if idx == 0 else []
            P.dma("POOL", "wquant", stg, wsrc, [f"d_w{wi + 1}"], [skey] + alias)
            P.emit("DVE", lambda q, stg=stg, tmp=tmp, wi=wi: q.tensor_scalar(
                tmp, stg, scp[:, 20 + 2 * wi:21 + 2 * wi], None, ALU.is_lt),
                reads=[skey, "scp"], writes=[mkey])
            tq = t1q if wi == 0 else t2q
            tkey = "tq0" if wi == 0 else "tq1"
            P.emit("DVE", lambda q, tq=tq, cch=cch, ss=ss, stg=stg, tmp=tmp,
                   wi=wi: q.scalar_tensor_tensor(
                       tq[:, cch * 4096 + ss * 512:cch * 4096 + ss * 512 + 512],
                       stg, scp[:, 21 + 2 * wi:22 + 2 * wi], tmp,
                       ALU.is_gt, ALU.subtract),
                   reads=[skey, mkey, "scp"], writes=[tkey])

        P.dma("POOL", "wload2", WO.rearrange("p (h n) -> p h n", h=8),
              dwo.rearrange("(h p) n -> p h n", p=128), ["d_wo"], ["wo"])
        P.dma("SP", "bload", cuml[:], db1[:], ["d_b1"], ["cuml"])
        P.dma("POOL", "bload2", b2b[:], db2[:], ["d_b2"], ["b2b"])

        def p4_A(t):
            oslot = t % 2
            otile = OTL[:, oslot * 1024:oslot * 1024 + 1024]
            P.dma("SP", "oload", otile.rearrange("p (h n) -> p h n", h=8),
                  d_o.rearrange("(h p) tt -> p h tt", p=128)
                  [:, :, t * 128:(t + 1) * 128],
                  ["d_o"], [f"otile{oslot}"], k=4)
            mms = []
            for half in range(2):
                for h8 in range(8):
                    mms.append(lambda q, h8=h8, half=half, otile=otile: q.matmul(
                        ps[half][:, 0:512], otile[:, h8 * 128:(h8 + 1) * 128],
                        WO[:, h8 * 1024 + half * 512:h8 * 1024 + half * 512 + 512],
                        start=(h8 == 0), stop=(h8 == 7)))
            P.group("PE", mms, reads=[f"otile{oslot}", "wo"],
                    writes=["ps0", "ps1"])

        p4_A(0)
        for t in range(PT):
            xa = u1b[:]
            P.dma("SP", "xtmload", xa, dxtm[t * 128:(t + 1) * 128, :],
                  ["d_xtm"], ["u1b"], k=4)
            for half in range(2):
                P.emit("DVE", lambda q, half=half, xa=xa: q.scalar_tensor_tensor(
                    remz[:, half * 512:(half + 1) * 512], ps[half][:, 0:512], 1.0,
                    xa[:, half * 512:(half + 1) * 512], ALU.bypass, ALU.add,
                    accum_out=sct[:, 16 + half:17 + half]),
                    reads=[f"ps{half}", "u1b"], writes=["remz", "sct"])
            if t + 1 < PT:
                p4_A(t + 1)
            P.emit("ACT", lambda q: q.activation(
                u1b[:], remz[:], AF.Square, bias=0.0, scale=1.0,
                accum_out=sct[:, 18:19]), reads=["remz"], writes=["u1b", "sct"])
            P.emit("DVE", lambda q: q.scalar_tensor_tensor(
                sct[:, 19:20], sct[:, 16:17], 1.0, sct[:, 17:18],
                ALU.bypass, ALU.add), reads=["sct"], writes=["sct"])
            P.emit("DVE", lambda q: q.tensor_scalar(
                sct[:, 20:21], sct[:, 19:20], 1.0 / D, None, ALU.mult),
                reads=["sct"], writes=["sct"])
            P.emit("DVE", lambda q: q.tensor_tensor(
                sct[:, 21:22], sct[:, 20:21], sct[:, 20:21], ALU.mult),
                reads=["sct"], writes=["sct"])
            P.emit("DVE", lambda q: q.scalar_tensor_tensor(
                sct[:, 22:23], sct[:, 18:19], 1.0 / D, sct[:, 21:22],
                ALU.mult, ALU.subtract), reads=["sct"], writes=["sct"])
            P.emit("ACT", lambda q: q.activation(
                sct[:, 23:24], sct[:, 22:23], AF.Sqrt, bias=cstv[:, 0:1], scale=1.0),
                reads=["sct", "cstv"], writes=["sct"])
            P.emit("DVE", lambda q: q.reciprocal(sct[:, 24:25], sct[:, 23:24]),
                   reads=["sct"], writes=["sct"])
            P.emit("DVE", lambda q: q.tensor_scalar(
                yf32[:], remz[:], sct[:, 20:21], sct[:, 24:25],
                ALU.subtract, ALU.mult), reads=["remz", "sct"], writes=["yf32"])
            P.dma("POOL", "ystore", d_yscr[t * 128:(t + 1) * 128, :], yf32[:],
                  ["yf32"], ["d_yscr"], k=4)
            P.emit("DVE", lambda q: q.tensor_reduce(
                sct[:, 25:26], yf32[:], AX.X, ALU.max, apply_absolute_value=True),
                reads=["yf32"], writes=["sct"])
            P.emit("ACT", lambda q: q.activation(
                u1b[:], yf32[:], AF.Square, bias=0.0, scale=1.0,
                accum_out=sct[:, 26:27]), reads=["yf32"], writes=["u1b", "sct"])
            P.emit("ACT", lambda q: q.activation(
                sct[:, 27:28], sct[:, 26:27], AF.Sqrt, bias=cstv[:, 0:1],
                scale=cstv[:, 1:2]), reads=["sct", "cstv"], writes=["sct"])
            P.emit("DVE", lambda q: q.reciprocal(sct[:, 28:29], sct[:, 27:28]),
                   reads=["sct"], writes=["sct"])
            P.emit("DVE", lambda q: q.tensor_tensor(
                sct[:, 29:30], sct[:, 28:29], sct[:, 25:26], ALU.mult),
                reads=["sct"], writes=["sct"])
            P.emit("DVE", lambda q: q.tensor_scalar(
                sct[:, 30:31], sct[:, 29:30], 1e-5, None, ALU.max),
                reads=["sct"], writes=["sct"])
            P.emit("DVE", lambda q: q.reciprocal(sct[:, 31:32], sct[:, 30:31]),
                   reads=["sct"], writes=["sct"])
            P.emit("DVE", lambda q: q.scalar_tensor_tensor(
                sct[:, 32:33], sct[:, 31:32], 127.0, sct[:, 28:29],
                ALU.mult, ALU.mult), reads=["sct"], writes=["sct"])
            P.emit("DVE", lambda q, t=t: q.scalar_tensor_tensor(
                scp[:, 4 + t:5 + t], sct[:, 30:31], 1.0, scp[:, 2:3],
                ALU.bypass, ALU.mult), reads=["sct"], writes=["scp"])
            P.emit("DVE", lambda q: q.tensor_scalar(
                u1b[:], yf32[:], sct[:, 32:33], None, ALU.mult),
                reads=["yf32", "sct"], writes=["u1b"])
            P.emit("DVE", lambda q: q.tensor_copy(
                yf32[:].bitcast(I32), u1b[:]),
                reads=["u1b"], writes=["yf32"])
            P.emit("DVE", lambda q: q.scalar_tensor_tensor(
                lneg[:, 0:1024], u1b[:], -0.5, yf32[:].bitcast(I32),
                ALU.add, ALU.is_gt),
                reads=["u1b", "yf32"], writes=["lneg"])
            P.emit("DVE", lambda q: q.scalar_tensor_tensor(
                lneg[:, 1024:2048], u1b[:], 0.5, yf32[:].bitcast(I32),
                ALU.add, ALU.is_lt),
                reads=["u1b", "yf32"], writes=["lneg"])
            P.emit("DVE", lambda q: q.scalar_tensor_tensor(
                lneg[:, 2048:3072], yf32[:].bitcast(I32), 1.0, lneg[:, 0:1024],
                ALU.bypass, ALU.add),
                reads=["yf32", "lneg"], writes=["lneg"])
            P.emit("DVE", lambda q: q.tensor_sub(
                N1S[:, 0:1024], lneg[:, 2048:3072], lneg[:, 1024:2048]),
                reads=["lneg"], writes=["n1tm"])
            trs = [(lambda q, cb=cb: q.transpose(
                pt6[:, cb * 128:(cb + 1) * 128], N1S[:, cb * 128:(cb + 1) * 128],
                c_iden[:])) for cb in range(8)]
            P.group("PE", trs, reads=["n1tm", "c_iden"], writes=["pt6"])
            P.emit("ACT", lambda q: q.activation(
                N1S[:, 1024:2048], pt6[:], AF.Copy, bias=0.0, scale=1.0),
                reads=["pt6"], writes=["n1st"])
            P.dma("POOL", "n1store",
                  d_n1scr.rearrange("(c t p) n -> t p c n", c=DC, p=128)
                  [t, :, :, :],
                  N1S[:, 1024:2048].rearrange("p (c n) -> p c n", c=8),
                  ["n1st"], ["d_n1scr"], k=4)
            for _ in range(8):
                _emit_tern()
        while tern_jobs:
            _emit_tern()

        P.barrier()
        WSP = wsl[:, 15360:16384]

        def p5_A(t):
            n1slot = t % 2
            n1a = N1D[:, n1slot * 1024:n1slot * 1024 + 1024]
            ya = YST[:, n1slot * 1024:n1slot * 1024 + 1024]
            P.dma("SP", "n1load", n1a.rearrange("p (c n) -> p c n", c=8),
                  d_n1scr.rearrange("(c t p) n -> t p c n", c=DC, p=128)
                  [t, :, :, :],
                  ["d_n1scr"], [f"n1a{n1slot}"], k=4)
            P.dma("SP", "yload", ya, d_yscr[t * 128:(t + 1) * 128, :],
                  ["d_yscr"], [f"ya{n1slot}"], k=4)
            for s in range(8):
                bank = s % 4
                mms = [(lambda q, c=c, s=s, bank=bank, n1a=n1a: q.matmul(
                    ps[bank][:, 0:512], n1a[:, c * 128:(c + 1) * 128],
                    t1q[:, c * 4096 + s * 512:c * 4096 + (s + 1) * 512],
                    start=(c == 0), stop=(c == DC - 1))) for c in range(DC)]
                P.group("PE", mms, reads=[f"n1a{n1slot}", "tq0"],
                        writes=[f"ps{bank}"])
                P.emit("DVE", lambda q, bank=bank, s=s, t=t:
                       q.scalar_tensor_tensor(
                           lneg[:, s * 512:(s + 1) * 512], ps[bank][:, 0:512],
                           scp[:, 4 + t:5 + t], cuml[:, s * 512:(s + 1) * 512],
                           ALU.mult, ALU.add),
                       reads=[f"ps{bank}", "scp", "cuml"],
                       writes=["lneg", "wst0", "wst1"])
                P.emit("ACT", lambda q, s=s: q.activation(
                    lneg[:, s * 512:(s + 1) * 512],
                    lneg[:, s * 512:(s + 1) * 512],
                    AF.Gelu, bias=0.0, scale=1.0),
                    reads=["lneg"], writes=["lneg"])

        def p5_B(t):
            for qq in range(4):
                qsl = slice(qq * 1024, (qq + 1) * 1024)
                P.emit("DVE", lambda q, qsl=qsl, qq=qq: q.tensor_reduce(
                    sct[:, 36 + qq:37 + qq], lneg[:, qsl], AX.X, ALU.max,
                    apply_absolute_value=True),
                    reads=["lneg"], writes=["sctB"])
                P.emit("ACT", lambda q, qsl=qsl, qq=qq: q.activation(
                    remz[:], lneg[:, qsl], AF.Square, bias=0.0, scale=1.0,
                    accum_out=sct[:, 60 + qq:61 + qq]),
                    reads=["lneg"], writes=["remz", "sctB"])
            P.emit("DVE", lambda q: q.tensor_reduce(
                sct[:, 42:43], sct[:, 36:40], AX.X, ALU.max),
                reads=["sctB"], writes=["sctB"])
            P.emit("DVE", lambda q: q.tensor_reduce(
                sct[:, 45:46], sct[:, 60:64], AX.X, ALU.add),
                reads=["sctB"], writes=["sctB"])
            P.emit("ACT", lambda q: q.activation(
                sct[:, 46:47], sct[:, 45:46], AF.Sqrt, bias=cstv[:, 0:1],
                scale=cstv[:, 2:3]), reads=["sctB", "cstv"], writes=["sctB"])
            P.emit("DVE", lambda q: q.reciprocal(sct[:, 47:48], sct[:, 46:47]),
                   reads=["sctB"], writes=["sctB"])
            P.emit("DVE", lambda q: q.tensor_tensor(
                sct[:, 48:49], sct[:, 47:48], sct[:, 42:43], ALU.mult),
                reads=["sctB"], writes=["sctB"])
            P.emit("DVE", lambda q: q.tensor_scalar(
                sct[:, 49:50], sct[:, 48:49], 1e-5, None, ALU.max),
                reads=["sctB"], writes=["sctB"])
            P.emit("DVE", lambda q: q.reciprocal(sct[:, 50:51], sct[:, 49:50]),
                   reads=["sctB"], writes=["sctB"])
            P.emit("DVE", lambda q: q.scalar_tensor_tensor(
                sct[:, 51:52], sct[:, 50:51], 127.0, sct[:, 47:48],
                ALU.mult, ALU.mult), reads=["sctB"], writes=["sctB"])
            P.emit("DVE", lambda q: q.scalar_tensor_tensor(
                sct[:, 52:53], sct[:, 49:50], 1.0, scp[:, 3:4],
                ALU.bypass, ALU.mult), reads=["sctB", "scp"], writes=["sctB"])
            for qq in range(4):
                qsl = slice(qq * 1024, (qq + 1) * 1024)
                P.emit("DVE", lambda q, qsl=qsl: q.tensor_scalar(
                    u1b[:], lneg[:, qsl], sct[:, 51:52], None, ALU.mult),
                    reads=["lneg", "sctB"], writes=["u1b"])
                P.emit("DVE", lambda q: q.tensor_copy(
                    yf32[:].bitcast(I32), u1b[:]),
                    reads=["u1b"], writes=["yf32"])
                P.emit("DVE", lambda q: q.scalar_tensor_tensor(
                    remz[:], u1b[:], -0.5, yf32[:].bitcast(I32),
                    ALU.add, ALU.is_gt),
                    reads=["u1b", "yf32"], writes=["remz"])
                P.emit("DVE", lambda q: q.scalar_tensor_tensor(
                    WSP, u1b[:], 0.5, yf32[:].bitcast(I32),
                    ALU.add, ALU.is_lt),
                    reads=["u1b", "yf32"], writes=["wspD", "wsg0", "wsg1"])
                P.emit("DVE", lambda q: q.scalar_tensor_tensor(
                    u1b[:], yf32[:].bitcast(I32), 1.0, remz[:],
                    ALU.bypass, ALU.add),
                    reads=["yf32", "remz"], writes=["u1b"])
                P.emit("DVE", lambda q, qsl=qsl: q.tensor_sub(
                    N2TM[:, qsl], u1b[:], WSP),
                    reads=["u1b", "wspD"], writes=["n2tm"])
            for bb in range(4):
                pv = pt6 if bb % 2 == 0 else pt7
                pk = f"pt{6 + bb % 2}"
                trs = [(lambda q, cb=bb * 8 + kk, kk=kk, pv=pv: q.transpose(
                    pv[:, kk * 128:(kk + 1) * 128],
                    N2TM[:, cb * 128:(cb + 1) * 128], c_iden[:]))
                    for kk in range(8)]
                P.group("PE", trs, reads=["n2tm", "c_iden"], writes=[pk])
                P.emit("ACT", lambda q, bb=bb, pv=pv: q.activation(
                    N2F[:, bb * 1024:(bb + 1) * 1024], pv[:],
                    AF.Copy, bias=0.0, scale=1.0),
                    reads=[pk], writes=["n2F"])

        def p5_C(t):
            n1slot = t % 2
            ya = YST[:, n1slot * 1024:n1slot * 1024 + 1024]
            mms = []
            for c2 in range(FC):
                for half in range(2):
                    mms.append(lambda q, c2=c2, half=half: q.matmul(
                        ps[4 + half][:, 0:512], N2F[:, c2 * 128:(c2 + 1) * 128],
                        t2q[:, c2 * 1024 + half * 512:
                            c2 * 1024 + (half + 1) * 512],
                        start=(c2 == 0), stop=(c2 == FC - 1)))
            P.group("PE", mms, reads=["n2F", "tq1"], writes=["ps4", "ps5"])
            for half in range(2):
                P.emit("DVE", lambda q, half=half: q.scalar_tensor_tensor(
                    remz[:, half * 512:(half + 1) * 512],
                    ps[4 + half][:, 0:512],
                    sct[:, 52:53], b2b[:, half * 512:(half + 1) * 512],
                    ALU.mult, ALU.add),
                    reads=[f"ps{4 + half}", "sctB", "b2b"], writes=["remz"])
            P.emit("DVE", lambda q, ya=ya, n1slot=n1slot: q.scalar_tensor_tensor(
                yf32[:], remz[:], 1.0, ya, ALU.bypass, ALU.add,
                accum_out=sct[:, 53:54]),
                reads=["remz", f"ya{n1slot}"], writes=["yf32", "sctC"])
            P.emit("ACT", lambda q: q.activation(
                u1b[:], yf32[:], AF.Square, bias=0.0, scale=1.0,
                accum_out=sct[:, 54:55]), reads=["yf32"],
                writes=["u1b", "sctC"])
            P.emit("DVE", lambda q: q.tensor_scalar(
                sct[:, 55:56], sct[:, 53:54], 1.0 / D, None, ALU.mult),
                reads=["sctC"], writes=["sctC"])
            P.emit("DVE", lambda q: q.tensor_tensor(
                sct[:, 56:57], sct[:, 55:56], sct[:, 55:56], ALU.mult),
                reads=["sctC"], writes=["sctC"])
            P.emit("DVE", lambda q: q.scalar_tensor_tensor(
                sct[:, 57:58], sct[:, 54:55], 1.0 / D, sct[:, 56:57],
                ALU.mult, ALU.subtract), reads=["sctC"], writes=["sctC"])
            P.emit("ACT", lambda q: q.activation(
                sct[:, 58:59], sct[:, 57:58], AF.Sqrt, bias=cstv[:, 0:1],
                scale=1.0),
                reads=["sctC", "cstv"], writes=["sctC"])
            P.emit("DVE", lambda q: q.reciprocal(sct[:, 59:60], sct[:, 58:59]),
                   reads=["sctC"], writes=["sctC"])
            P.emit("DVE", lambda q: q.tensor_scalar(
                remz[:], yf32[:], sct[:, 55:56], sct[:, 59:60],
                ALU.subtract, ALU.mult),
                reads=["yf32", "sctC"], writes=["remz"])
            P.dma("POOL", "outstore", dout[t * 128:(t + 1) * 128, :], remz[:],
                  ["remz"], ["d_out"], k=4)

        p5_A(0)
        for t in range(PT):
            p5_B(t)
            if t + 1 < PT:
                p5_A(t + 1)
            p5_C(t)

        P.barrier()
    return nc


def _consts():
    import ml_dtypes
    bf = ml_dtypes.bfloat16
    return {
        "iden": np.eye(128, dtype=np.float32).astype(bf),
        "idenf": np.eye(128, dtype=np.float32),
        "tri": np.vstack([np.triu(np.ones((64, 64), dtype=np.float32))] * 2).astype(bf),
        "onescol": np.ones((128, 1), dtype=np.float32).astype(bf),
    }


def kernel(L0_attn_x, L0_attn_Wq, L0_attn_Wk, L0_attn_Wv, L0_attn_Wg,
           L0_attn_Wo, L0_ffn_w1, L0_ffn_b1, L0_ffn_w2, L0_ffn_b2):
    x = np.ascontiguousarray(np.asarray(L0_attn_x, dtype=np.float32))
    Wq = np.ascontiguousarray(np.asarray(L0_attn_Wq, dtype=np.float32))
    Wk = np.ascontiguousarray(np.asarray(L0_attn_Wk, dtype=np.float32))
    Wv = np.ascontiguousarray(np.asarray(L0_attn_Wv, dtype=np.float32))
    Wg = np.ascontiguousarray(np.asarray(L0_attn_Wg, dtype=np.float32))
    Wo = np.ascontiguousarray(np.asarray(L0_attn_Wo, dtype=np.float32))
    w1t = np.ascontiguousarray(np.asarray(L0_ffn_w1, dtype=np.float32).T)
    w2t = np.ascontiguousarray(np.asarray(L0_ffn_w2, dtype=np.float32).T)
    b1 = np.ascontiguousarray(
        np.broadcast_to(np.asarray(L0_ffn_b1, dtype=np.float32), (128, 4096)))
    b2 = np.ascontiguousarray(
        np.broadcast_to(np.asarray(L0_ffn_b2, dtype=np.float32), (128, 1024)))
    B, T, D = x.shape
    TW = T // 2
    PFX = 256
    cst = _consts()
    in_maps = []
    for c in range(8):
        b, par = c // 2, c % 2
        xT = x[b].T
        if par == 0:
            xdm = np.hstack([np.zeros((D, PFX), np.float32), xT[:, 0:TW]])
        else:
            xdm = xT[:, TW - PFX:T]
        m = {
            "xdm": np.ascontiguousarray(xdm),
            "xtm": np.ascontiguousarray(x[b, par * TW:(par + 1) * TW]),
            "wq": Wq, "wk": Wk, "wg": Wg, "wv": Wv,
            "wo": Wo, "w1t": w1t, "w2t": w2t, "b1": b1, "b2": b2,
        }
        m.update(cst)
        in_maps.append(m)
    nc = build_program(TW=TW, PFX=PFX)
    res = run_bass_kernel_spmd(nc, in_maps, core_ids=list(range(8)))
    out = np.empty((B, T, D), dtype=np.float32)
    for c in range(8):
        b, par = c // 2, c % 2
        out[b, par * TW:(par + 1) * TW] = np.asarray(res.results[c]["out"])
    return out


# version 25 (final)
# speedup vs baseline: 1.0596x; 1.0513x over previous
import numpy as np
from contextlib import ExitStack

import concourse.bass as bass
import concourse.mybir as mybir
from concourse.bass_utils import run_bass_kernel_spmd

F32 = mybir.dt.float32
I32 = mybir.dt.int32
BF16 = mybir.dt.bfloat16
AF = mybir.ActivationFunctionType
ALU = mybir.AluOpType
AX = mybir.AxisListType

ENGS = ("PE", "DVE", "ACT", "POOL", "SP")
_P2SUB = 99
_NHEADS = 99
_NCHDBG = 4


class Prog:

    def __init__(self, nc, ctx):
        self.nc = nc
        self.ctx = ctx
        self.eng = {"PE": nc.tensor, "DVE": nc.vector, "ACT": nc.scalar,
                    "POOL": nc.gpsimd, "SP": nc.sync}
        self.engsem = {}
        self.pos = {}
        for e in ("PE", "DVE", "ACT", "POOL"):
            self.engsem[e] = ctx.enter_context(nc.semaphore("qs_" + e))
            self.pos[e] = 0
        self.streams = {}
        self.satisfied = {e: {} for e in ENGS}
        self.res = {}

    def stream(self, name, k=2):
        if name not in self.streams:
            sems = []
            for i in range(k):
                h = self.ctx.enter_context(self.nc.semaphore(f"dm_{name}{i}"))
                sems.append([f"dm_{name}{i}", h, 0])
            self.streams[name] = {"sems": sems, "rr": 0}
        return self.streams[name]

    def _r(self, key):
        if key not in self.res:
            self.res[key] = {"w": None, "r": []}
        return self.res[key]

    def _wait(self, ename, tokens):
        best = {}
        for (sname, h, v) in tokens:
            cur = best.get(sname)
            if cur is None or v > cur[1]:
                best[sname] = (h, v)
        q = self.eng[ename]
        for sname in sorted(best):
            h, v = best[sname]
            if ename == "PE" and sname == "qs_PE":
                continue
            if v <= self.satisfied[ename].get(sname, -1):
                continue
            q.wait_ge(h, v)
            self.satisfied[ename][sname] = v

    def _deps(self, reads, writes):
        toks = []
        for k in reads:
            r = self._r(k)
            if r["w"] is not None:
                toks.append(r["w"])
        for k in writes:
            r = self._r(k)
            if r["w"] is not None:
                toks.append(r["w"])
            toks.extend(r["r"])
        return toks

    def _commit(self, tok, reads, writes):
        for k in writes:
            r = self._r(k)
            r["w"] = tok
            r["r"] = []
        for k in reads:
            if k not in writes:
                self._r(k)["r"].append(tok)

    def emit(self, ename, fn, reads=(), writes=()):
        self._wait(ename, self._deps(reads, writes))
        instr = fn(self.eng[ename])
        self.pos[ename] += 1
        instr.then_inc(self.engsem[ename], 1)
        self._commit(("qs_" + ename, self.engsem[ename], self.pos[ename]),
                     reads, writes)

    def group(self, ename, fns, reads=(), writes=()):
        self._wait(ename, self._deps(reads, writes))
        instr = None
        for fn in fns:
            instr = fn(self.eng[ename])
        self.pos[ename] += 1
        instr.then_inc(self.engsem[ename], 1)
        self._commit(("qs_" + ename, self.engsem[ename], self.pos[ename]),
                     reads, writes)

    def dma(self, ename, stream_name, out, in_, reads=(), writes=(), k=2, **kw):
        st = self.stream(stream_name, k)
        slot = st["sems"][st["rr"] % len(st["sems"])]
        st["rr"] += 1
        toks = self._deps(reads, writes)
        if slot[2] > 0:
            toks.append((slot[0], slot[1], slot[2]))
        self._wait(ename, toks)
        instr = self.eng[ename].dma_start(out=out, in_=in_, **kw)
        slot[2] += 16
        instr.then_inc(slot[1], 16)
        self._commit((slot[0], slot[1], slot[2]), reads, writes)

    def barrier(self):
        marks = [("qs_" + e, self.engsem[e], self.pos[e])
                 for e in ("PE", "DVE", "ACT", "POOL")]
        for st in self.streams.values():
            for sname, h, tot in st["sems"]:
                if tot > 0:
                    marks.append((sname, h, tot))
        for e in ENGS:
            self._wait(e, marks)


def build_program(TW=2048, PFX=512):
    D = 1024
    H8 = 8
    C = 64
    FF = 4096
    DC = D // 128
    TA = TW + PFX
    NCA = TA // C
    NCP = PFX // C
    NTA = TA // 128
    NTP = PFX // 128
    PT = TW // 128
    FC = FF // 128

    nc = bass.Bass("TRN2", target_bir_lowering=False)

    dxdm = nc.dram_tensor("xdm", [D, TA], F32, kind="ExternalInput").ap()
    dxtm = nc.dram_tensor("xtm", [TW, D], F32, kind="ExternalInput").ap()
    dwq = nc.dram_tensor("wq", [D, H8 * 128], F32, kind="ExternalInput").ap()
    dwk = nc.dram_tensor("wk", [D, H8 * 128], F32, kind="ExternalInput").ap()
    dwg = nc.dram_tensor("wg", [D, H8 * 128], F32, kind="ExternalInput").ap()
    dwv = nc.dram_tensor("wv", [D, H8 * 128], F32, kind="ExternalInput").ap()
    dwo = nc.dram_tensor("wo", [H8 * 128, D], F32, kind="ExternalInput").ap()
    dw1t = nc.dram_tensor("w1t", [D, FF], F32, kind="ExternalInput").ap()
    dw2t = nc.dram_tensor("w2t", [FF, D], F32, kind="ExternalInput").ap()
    db1 = nc.dram_tensor("b1", [128, FF], F32, kind="ExternalInput").ap()
    db2 = nc.dram_tensor("b2", [128, D], F32, kind="ExternalInput").ap()
    diden = nc.dram_tensor("iden", [128, 128], BF16, kind="ExternalInput").ap()
    dtri = nc.dram_tensor("tri", [128, C], BF16, kind="ExternalInput").ap()
    dones = nc.dram_tensor("onescol", [128, 1], BF16, kind="ExternalInput").ap()
    didenf = nc.dram_tensor("idenf", [128, 128], F32, kind="ExternalInput").ap()
    dout = nc.dram_tensor("out", [TW, D], F32, kind="ExternalOutput").ap()

    d_o = nc.dram_tensor("od", [H8 * 128, TW], BF16, kind="Internal").ap()
    d_yscr = nc.dram_tensor("yscr", [TW, D], BF16, kind="Internal").ap()
    d_n1scr = nc.dram_tensor("n1scr", [DC * PT * 128, 128], BF16, kind="Internal").ap()
    d_qscr = nc.dram_tensor("qscr", [H8 * PT * 128, 128], F32, kind="Internal").ap()

    with ExitStack() as ctx:
        e = ctx.enter_context
        biga = e(nc.sbuf_tensor("biga", [128, 32768], BF16))
        bigb = e(nc.sbuf_tensor("bigb", [128, 32768], BF16))
        wsl = e(nc.sbuf_tensor("wsl", [128, 16384], BF16))
        lneg = e(nc.sbuf_tensor("lneg", [128, 4096], F32))
        cuml = e(nc.sbuf_tensor("cuml", [128, 4096], F32))
        remz = e(nc.sbuf_tensor("remz", [128, 1024], F32))
        yf32 = e(nc.sbuf_tensor("yf32", [128, 1024], F32))
        u1b = e(nc.sbuf_tensor("u1b", [128, 1024], F32))
        pcf = e(nc.sbuf_tensor("pcf", [128, NCA], F32))
        nrm = e(nc.sbuf_tensor("nrm", [128, 8 + 3 * NTA], F32))
        sct = e(nc.sbuf_tensor("sct", [128, 64], F32))
        scp = e(nc.sbuf_tensor("scp", [128, 24], F32))
        wmp = e(nc.sbuf_tensor("wmp", [128, 18], F32))
        wrow = e(nc.sbuf_tensor("wrow", [1, 12], F32))
        khts = e(nc.sbuf_tensor("khts", [128, 128], BF16))
        ssb = e(nc.sbuf_tensor("ssb", [128, 256], F32))
        sbf = e(nc.sbuf_tensor("sbf", [128, 128], BF16))
        ostg = e(nc.sbuf_tensor("ostg", [128, 128], BF16))
        amsk = e(nc.sbuf_tensor("amsk", [128, 64], BF16))
        c_iden = e(nc.sbuf_tensor("c_iden", [128, 128], BF16))
        c_idenf = e(nc.sbuf_tensor("c_idenf", [128, 128], F32))
        c_tri = e(nc.sbuf_tensor("c_tri", [128, C], BF16))
        c_ones = e(nc.sbuf_tensor("c_ones", [128, 1], BF16))
        cstv = e(nc.sbuf_tensor("cstv", [128, 3], F32))

        ps = [e(nc.psum_tensor(f"ps{i}", [128, 512], F32)) for i in range(6)]
        pt6 = e(nc.psum_tensor("pt6", [128, 1024], BF16))
        pt7 = e(nc.psum_tensor("pt7", [128, 1024], BF16))

        block = e(nc.Block())
        P = Prog(nc, ctx)

        vt = biga[:, 0:NTA * 1024]
        qstg = biga[:, NTA * 1024:NTA * 1024 + 2048].bitcast(F32)
        XB0 = NTA * 1024 + 2048
        assert XB0 + 5 * TA <= 32768 and 8192 + 3 * TA <= 16384
        xres = []
        for c in range(8):
            if c >= 3:
                xres.append(biga[:, XB0 + (c - 3) * TA:XB0 + (c - 2) * TA])
            else:
                xres.append(wsl[:, 8192 + c * TA:8192 + (c + 1) * TA])
        XR = [f"xr{c}" for c in range(8)]
        t2q = biga
        F_k = bigb[:, 0:2 * TA].bitcast(F32)
        F_lng = bigb[:, 2 * TA:2 * TA + 1024].bitcast(F32)
        B1 = 2 * TA + 1024
        F_cl = bigb[:, B1:B1 + 2 * TA].bitcast(F32)
        F_e1 = bigb[:, B1 + 2 * TA:B1 + 4 * TA].bitcast(F32)
        F_e2 = bigb[:, B1 + 4 * TA:B1 + 6 * TA].bitcast(F32)
        B0 = B1 + 6 * TA
        A_ksq = bigb[:, B0:B0 + TA]
        A_khP = [bigb[:, B0 + TA:B0 + 2 * TA],
                 bigb[:, B0 + 2 * TA:B0 + 3 * TA]]
        A_qhP = [bigb[:, B0 + 3 * TA:B0 + 3 * TA + TW],
                 bigb[:, B0 + 3 * TA + TW:B0 + 3 * TA + 2 * TW]]
        assert B0 + 3 * TA + 2 * TW <= 32768
        t1q = bigb
        WQf = wsl[:, 0:8192]
        WVf = wsl[:, 0:8192]
        WKh = wsl[:, 0:1024]
        WGh = wsl[:, 1024:2048]
        WO = wsl[:, 0:8192]
        OTL = wsl[:, 8192:12288]
        N1S = wsl[:, 12288:14336]
        b2b = wsl[:, 14336:15360]
        N2TM = wsl[:, 0:4096]
        N2F = wsl[:, 4096:8192]
        N1D = wsl[:, 8192:10240]
        YST = wsl[:, 10240:12288]

        P.emit("DVE", lambda q: q.memset(cstv[:, 0:1], 1e-5), writes=["cstv"])
        P.emit("DVE", lambda q: q.memset(cstv[:, 1:2], 1.0 / D), writes=["cstv"])
        P.emit("DVE", lambda q: q.memset(cstv[:, 2:3], 1.0 / FF), writes=["cstv"])
        P.dma("SP", "consts", c_iden[:], diden[:], ["d_iden"], ["c_iden"], k=4)
        P.dma("SP", "consts", c_idenf[:], didenf[:], ["d_idenf"], ["c_idenf"], k=4)
        P.dma("SP", "consts", c_tri[:], dtri[:], ["d_tri"], ["c_tri"], k=4)
        P.dma("SP", "consts", c_ones[:], dones[:], ["d_ones"], ["c_ones"], k=4)
        P.dma("POOL", "wload", WVf.rearrange("p (c n) -> p c n", n=1024),
              dwv.rearrange("(c p) n -> p c n", p=128),
              ["d_wv"], ["sl_wv", "sl_wq"], k=4)
        for c in range(8):
            P.dma("POOL", "xload", xres[c][:],
                  dxdm.rearrange("(c p) tt -> p c tt", p=128)[:, c, :],
                  ["d_xdm"], [f"xr{c}"], k=4)

        for t in range(NTA):
            for half in range(2):
                mv = [(lambda q, c=c, t=t, half=half: q.matmul(
                    ps[half][:, 0:512], xres[c][:, t * 128:(t + 1) * 128],
                    WVf[:, c * 1024 + half * 512:c * 1024 + half * 512 + 512],
                    start=(c == 0), stop=(c == DC - 1))) for c in range(DC)]
                P.group("PE", mv, reads=XR + ["sl_wv"], writes=[f"ps{half}"])
                P.emit("ACT", lambda q, t=t, half=half: q.activation(
                    vt[:, t * 1024 + half * 512:t * 1024 + half * 512 + 512],
                    ps[half][:, 0:512], AF.Copy, bias=0.0, scale=1.0),
                    reads=[f"ps{half}"], writes=["vt"])
        P.dma("POOL", "wload", WQf.rearrange("p (c n) -> p c n", n=1024),
              dwq.rearrange("(c p) n -> p c n", p=128),
              ["d_wq"], ["sl_wq", "sl_wv"], k=4)
        def q_proj(t):
            qslot = t % 2
            qa = lneg[:, qslot * 1024:qslot * 1024 + 1024]
            for half in range(2):
                mq = [(lambda q, c=c, t=t, half=half: q.matmul(
                    ps[2 + half][:, 0:512], xres[c][:, t * 128:(t + 1) * 128],
                    WQf[:, c * 1024 + half * 512:c * 1024 + half * 512 + 512],
                    start=(c == 0), stop=(c == DC - 1))) for c in range(DC)]
                P.group("PE", mq, reads=XR + ["sl_wq"], writes=[f"ps{2 + half}"])
                P.emit("ACT", lambda q, qa=qa, half=half: q.activation(
                    qa[:, half * 512:half * 512 + 512], ps[2 + half][:, 0:512],
                    AF.Copy, bias=0.0, scale=1.0),
                    reads=[f"ps{2 + half}"], writes=[f"qtm{qslot}"])

        def q_tail(t):
            tw = t - NTP
            qslot = t % 2
            qa = lneg[:, qslot * 1024:qslot * 1024 + 1024]
            sc0 = qslot * 24
            nr0 = qslot * 8
            sk, nk = f"sct{qslot}", f"nrmq{qslot}"
            for h8 in range(H8):
                P.emit("DVE", lambda q, qa=qa, h8=h8: q.scalar_tensor_tensor(
                    u1b[:, 0:128], qa[:, h8 * 128:(h8 + 1) * 128], 1.0,
                    qa[:, h8 * 128:(h8 + 1) * 128], ALU.bypass, ALU.mult,
                    accum_out=nrm[:, nr0 + h8:nr0 + h8 + 1]),
                    reads=[f"qtm{qslot}"], writes=["u1b", nk])
            P.emit("ACT", lambda q: q.activation(
                sct[:, sc0:sc0 + H8], nrm[:, nr0:nr0 + H8], AF.Sqrt,
                bias=0.0, scale=1.0), reads=[nk], writes=[sk])
            P.emit("DVE", lambda q: q.tensor_scalar(
                sct[:, sc0 + 8:sc0 + 8 + H8], sct[:, sc0:sc0 + H8],
                1e-12, None, ALU.max), reads=[sk], writes=[sk])
            P.emit("DVE", lambda q: q.reciprocal(
                sct[:, sc0 + 16:sc0 + 16 + H8], sct[:, sc0 + 8:sc0 + 8 + H8]),
                reads=[sk], writes=[sk])
            qb = cuml[:, qslot * 1024:qslot * 1024 + 1024]
            for bb in range(2):
                pv = ps[bb]
                for k4 in range(4):
                    h8 = bb * 4 + k4
                    qh_t = qb[:, h8 * 128:(h8 + 1) * 128]
                    P.emit("DVE", lambda q, qa=qa, h8=h8, qh_t=qh_t, sc0=sc0:
                           q.tensor_scalar(
                               qh_t, qa[:, h8 * 128:(h8 + 1) * 128],
                               sct[:, sc0 + 16 + h8:sc0 + 17 + h8], None,
                               ALU.mult),
                           reads=[f"qtm{qslot}", sk],
                           writes=[f"qhatb{qslot}"])
                    P.emit("PE", lambda q, qh_t=qh_t, k4=k4, pv=pv: q.transpose(
                        pv[:, k4 * 128:(k4 + 1) * 128], qh_t, c_idenf[:]),
                        reads=[f"qhatb{qslot}", "c_idenf"], writes=[f"ps{bb}"])
                qsg = qstg[:, bb * 512:bb * 512 + 512]
                P.emit("DVE", lambda q, pv=pv, qsg=qsg: q.tensor_copy(
                    qsg, pv[:, 0:512]),
                    reads=[f"ps{bb}"], writes=[f"qstg{bb}"])
                P.dma("SP", "qstore",
                      d_qscr.rearrange("(h t p) n -> t p h n", h=H8, p=128)
                      [tw, :, bb * 4:bb * 4 + 4, :],
                      qsg.rearrange("p (h n) -> p h n", h=4),
                      [f"qstg{bb}"], ["d_qscr"], k=4)

        q_proj(NTP)
        for t in range(NTP, NTA):
            if t + 1 < NTA:
                q_proj(t + 1)
            q_tail(t)

        mean_jobs = []
        for wi_, dsrc_ in enumerate((dw1t, dw2t)):
            for cch_ in range(8):
                mean_jobs.append((wi_, cch_, dsrc_))

        def _emit_mean_chunk():
            if not mean_jobs:
                return
            wi, cch, dsrc = mean_jobs.pop(0)
            slb = (wi * 8 + cch) % 2
            buf = lneg if slb == 0 else cuml
            bkey = "lneg" if slb == 0 else "cuml"
            alias = ["qtm0", "qtm1"] if slb == 0 else ["qhatb"]
            if wi == 0:
                wsrc = dsrc.rearrange("(c p) n -> p c n", p=128)[:, cch, :]
                dst = buf[:]
            else:
                wsrc = dsrc.rearrange("(c p) n -> p c n", p=128)                     [:, cch * 4:cch * 4 + 4, :]
                dst = buf[:].rearrange("p (c n) -> p c n", c=4)
            P.dma("SP", "wmean", dst, wsrc, [f"d_w{wi + 1}"], [bkey] + alias)
            P.emit("DVE", lambda q, buf=buf, wi=wi, cch=cch: q.tensor_reduce(
                wmp[:, wi * 8 + cch:wi * 8 + cch + 1], buf[:], AX.X, ALU.add,
                apply_absolute_value=True),
                reads=[bkey], writes=["wmp"])

        prep_q = []
        NHD = min(H8, _NHEADS)

        def build_prep(h8):
            par = h8 % 2
            kh_a, qh_a = A_khP[par], A_qhP[par]
            kkey, qkey = f"khb{par}", f"qhb{par}"
            rks = sct[:, 24 + par * NTA:24 + (par + 1) * NTA]
            pcf_a = pcf[:, 0:NCA] if par == 0 else nrm[:, 8:8 + NCA]
            pkey = f"pcf{par}"
            E = prep_q.append
            E(lambda h8=h8: P.dma(
                "POOL", "wkh", WKh.rearrange("p (c n) -> p c n", n=128),
                dwk.rearrange("(c p) n -> p c n", p=128)
                [:, :, h8 * 128:(h8 + 1) * 128],
                ["d_wk"], ["wkh", "sl_wq"], k=2))
            E(lambda h8=h8: P.dma(
                "POOL", "wgh", WGh.rearrange("p (c n) -> p c n", n=128),
                dwg.rearrange("(c p) n -> p c n", p=128)
                [:, :, h8 * 128:(h8 + 1) * 128],
                ["d_wg"], ["wgh", "sl_wq"], k=2))
            for s0 in range(0, TA, 512):
                gw = min(512, TA - s0)
                sl = slice(s0, s0 + gw)
                mk = [(lambda q, c=c, s0=s0, gw=gw: q.matmul(
                    ps[0][:, 0:gw], WKh[:, c * 128:(c + 1) * 128],
                    xres[c][:, s0:s0 + gw],
                    start=(c == 0), stop=(c == DC - 1))) for c in range(DC)]
                mg = [(lambda q, c=c, s0=s0, gw=gw: q.matmul(
                    ps[1][:, 0:gw], WGh[:, c * 128:(c + 1) * 128],
                    xres[c][:, s0:s0 + gw],
                    start=(c == 0), stop=(c == DC - 1))) for c in range(DC)]
                E(lambda mk=mk: P.group("PE", mk, reads=XR + ["wkh"],
                                        writes=["ps0"]))
                E(lambda sl=sl, gw=gw: P.emit("ACT", lambda q: q.activation(
                    F_k[:, sl], ps[0][:, 0:gw], AF.Copy, bias=0.0, scale=1.0),
                    reads=["ps0"], writes=["kdk"]))
                E(lambda sl=sl, gw=gw: P.emit("ACT", lambda q: q.activation(
                    A_ksq[:, sl], ps[0][:, 0:gw], AF.Square, bias=0.0,
                    scale=1.0), reads=["ps0"], writes=["ksq"]))
                E(lambda mg=mg: P.group("PE", mg, reads=XR + ["wgh"],
                                        writes=["ps1"]))
                E(lambda gw=gw: P.emit("ACT", lambda q: q.activation(
                    remz[:, 0:gw], ps[1][:, 0:gw], AF.Exp, bias=0.0,
                    scale=-1.0), reads=["ps1"], writes=["remz"]))
                E(lambda gw=gw: P.emit("ACT", lambda q: q.activation(
                    F_lng[:, 0:gw], remz[:, 0:gw], AF.Ln, bias=1.0,
                    scale=1.0), reads=["remz"], writes=["lng"]))
                for cc in range(gw // C):
                    ci0 = s0 // C + cc
                    E(lambda cc=cc, ci0=ci0: P.emit(
                        "DVE", lambda q: q.tensor_tensor_scan(
                            F_cl[:, ci0 * C:(ci0 + 1) * C],
                            F_lng[:, cc * C:(cc + 1) * C],
                            F_lng[:, cc * C:(cc + 1) * C],
                            0.0, ALU.add, ALU.bypass),
                        reads=["lng"], writes=["cuml", "qhat"]))
            nmm = [(lambda q, t2=t2: q.matmul(
                ps[0][:, t2:t2 + 1], A_ksq[:, t2 * 128:(t2 + 1) * 128],
                c_ones[:], start=True, stop=True,
                skip_group_check=True)) for t2 in range(NTA)]
            E(lambda nmm=nmm: P.group("PE", nmm, reads=["ksq", "c_ones"],
                                      writes=["ps0"]))
            E(lambda: P.emit("ACT", lambda q: q.activation(
                nrm[:, 8 + NCA:8 + NCA + NTA], ps[0][:, 0:NTA], AF.Sqrt,
                bias=0.0, scale=1.0), reads=["ps0"],
                writes=["nrmk", "nrmq1"]))
            E(lambda: P.emit("DVE", lambda q: q.tensor_scalar(
                nrm[:, 8 + NCA:8 + NCA + NTA], nrm[:, 8 + NCA:8 + NCA + NTA],
                1e-12, None, ALU.max), reads=["nrmk"], writes=["nrmk"]))
            E(lambda rks=rks, par=par: P.emit("DVE", lambda q: q.reciprocal(
                rks, nrm[:, 8 + NCA:8 + NCA + NTA]),
                reads=["nrmk"], writes=[f"rk{par}", "sct0", "sct1"]))
            E(lambda pcf_a=pcf_a, pkey=pkey: P.emit(
                "ACT", lambda q: q.activation(
                    pcf_a, F_cl[:, C - 1:TA:C], AF.Exp, bias=0.0, scale=-1.0),
                reads=["cuml"], writes=[pkey, "nrmq1"]))
            for s0 in range(0, TA, 512):
                sl = slice(s0, min(s0 + 512, TA))
                E(lambda sl=sl: P.emit("ACT", lambda q: q.activation(
                    F_e1[:, sl], F_cl[:, sl], AF.Exp, bias=0.0, scale=-1.0),
                    reads=["cuml"], writes=["e1"]))
                E(lambda sl=sl: P.emit("ACT", lambda q: q.activation(
                    F_e2[:, sl], F_cl[:, sl], AF.Exp, bias=0.0, scale=1.0),
                    reads=["cuml"], writes=["e2"]))
            for s0 in range(0, TA, 512):
                sl = slice(s0, min(s0 + 512, TA))
                E(lambda sl=sl, kh_a=kh_a, kkey=kkey: P.emit(
                    "POOL", lambda q: q.tensor_tensor(
                        kh_a[:, sl], F_k[:, sl], F_e2[:, sl], ALU.mult),
                    reads=["kdk", "e2"], writes=[kkey]))
            E(lambda h8=h8, pkey=pkey: P.dma(
                "ACT", "qload",
                F_cl[:, 0:TW].rearrange("p (t n) -> p t n", n=128),
                d_qscr.rearrange("(h t p) n -> h p t n", h=H8, p=128)
                [h8, :, :, :],
                ["d_qscr", "e1", pkey], ["qhat", "cuml"], k=2))
            for g in range(max(1, TW // 512)):
                gsl = min(512, TW)
                sl = slice(g * gsl, (g + 1) * gsl)
                E(lambda sl=sl, qh_a=qh_a, qkey=qkey: P.emit(
                    "POOL", lambda q: q.tensor_tensor(
                        qh_a[:, sl], F_cl[:, sl],
                        F_e1[:, PFX + sl.start:PFX + sl.stop], ALU.mult),
                    reads=["qhat", "e1"], writes=[qkey]))

        build_prep(0)
        while prep_q:
            prep_q.pop(0)()
        for h8 in range(NHD):
            hpar = h8 % 2
            KH, QH = A_khP[hpar], A_qhP[hpar]
            kkey, qkey = f"khb{hpar}", f"qhb{hpar}"
            rkkey, pkey = f"rk{hpar}", f"pcf{hpar}"
            rks = sct[:, 24 + hpar * NTA:24 + (hpar + 1) * NTA]
            pcf_a = pcf[:, 0:NCA] if hpar == 0 else nrm[:, 8:8 + NCA]
            P.emit("DVE", lambda q: q.memset(ssb[:, 0:128], 0.0),
                   reads=[], writes=["S0"])

            def _sc(ci):
                sl = slice(ci * C, (ci + 1) * C)
                par = ci % 2
                tile = ci // 2
                row0 = par * 64
                return dict(
                    sl=sl, q_sl=slice(sl.start - PFX, sl.stop - PFX), par=par,
                    tile=tile, row0=row0, cur=par, nxt=(ci + 1) % 2,
                    psK=pt6 if par == 0 else pt7,
                    psO=ps[2 + par], psS=ps[4 + par],
                    kK=f"pt{6 + par}", kO=f"ps{2 + par}", kS=f"ps{4 + par}",
                    v_c=vt[row0:row0 + 64,
                           tile * 1024 + h8 * 128:tile * 1024 + (h8 + 1) * 128],
                    rk_c=rks[row0:row0 + 64, tile:tile + 1],
                    tp=(0, row0) if row0 else None)

            def emit_AB(ci):
                s = _sc(ci)
                par, row0, rk_c = s["par"], s["row0"], s["rk_c"]
                P.emit("PE", lambda q, sl=s["sl"], psK=s["psK"], row0=row0,
                       tp=s["tp"]: q.transpose(
                    psK[row0:row0 + 64, 0:128], KH[:, sl], c_iden[:],
                    tile_position=tp),
                    reads=[kkey, "c_iden"], writes=[s["kK"]])
                P.emit("ACT", lambda q, psK=s["psK"], row0=row0, rk_c=rk_c:
                       q.activation(
                           khts[row0:row0 + 64, 0:128],
                           psK[row0:row0 + 64, 0:128],
                           AF.Copy, bias=0.0, scale=rk_c),
                       reads=[s["kK"], rkkey], writes=[f"khts{par}"])
                if ci >= NCP:
                    P.emit("PE", lambda q, sl=s["sl"], q_sl=s["q_sl"],
                           psS=s["psS"], row0=row0, tp=s["tp"]: q.matmul(
                        psS[row0:row0 + 64, 0:64], KH[:, sl], QH[:, q_sl],
                        start=True, stop=True, tile_position=tp),
                        reads=[kkey, qkey], writes=[s["kS"]])
                    P.emit("DVE", lambda q, psS=s["psS"], row0=row0, rk_c=rk_c:
                           q.scalar_tensor_tensor(
                               amsk[row0:row0 + 64, 0:64],
                               psS[row0:row0 + 64, 0:64],
                               rk_c, c_tri[row0:row0 + 64, :],
                               ALU.mult, ALU.mult),
                           reads=[s["kS"], rkkey, "c_tri"],
                           writes=[f"amsk{par}"])

            def emit_C(ci):
                s = _sc(ci)
                par, row0 = s["par"], s["row0"]
                cur, nxt = s["cur"], s["nxt"]
                psO, kO = s["psO"], s["kO"]
                if ci >= NCP:
                    P.emit("ACT", lambda q, cur=cur: q.activation(
                        sbf[:], ssb[:, cur * 128:cur * 128 + 128],
                        AF.Copy, bias=0.0, scale=1.0),
                        reads=[f"S{cur}"], writes=["Sbf"])
                    P.group("PE", [
                        lambda q, psO=psO, row0=row0, v_c=s["v_c"]: q.matmul(
                            psO[:, 0:64], v_c, amsk[row0:row0 + 64, 0:64],
                            start=True, stop=False),
                        lambda q, psO=psO, q_sl=s["q_sl"]: q.matmul(
                            psO[:, 0:64], sbf[:],
                            QH[:, q_sl], start=False, stop=True),
                        lambda q, psO=psO, row0=row0, v_c=s["v_c"]: q.matmul(
                            psO[:, 64:192], khts[row0:row0 + 64, 0:128],
                            v_c, start=True, stop=True),
                    ], reads=[f"amsk{par}", "vt", "Sbf", qkey, f"khts{par}"],
                        writes=[kO])
                    wc = ci - NCP
                    osl = wc % 2
                    P.emit("ACT", lambda q, psO=psO, osl=osl: q.activation(
                        ostg[:, osl * 64:osl * 64 + 64], psO[:, 0:64],
                        AF.Copy, bias=0.0, scale=1.0),
                        reads=[kO], writes=[f"ostg{osl}"])
                    if osl == 1 or ci == NCA - 1:
                        tloc = (wc - osl) * C
                        P.dma("ACT", "ostore",
                              d_o[h8 * 128:(h8 + 1) * 128,
                                  tloc:tloc + (osl + 1) * C],
                              ostg[:, 0:(osl + 1) * 64],
                              [f"ostg{k}" for k in range(osl + 1)],
                              ["d_o"], k=4)
                else:
                    P.emit("PE", lambda q, psO=psO, row0=row0, v_c=s["v_c"]:
                           q.matmul(
                               psO[:, 64:192], khts[row0:row0 + 64, 0:128],
                               v_c, start=True, stop=True),
                           reads=["vt", f"khts{par}"], writes=[kO])
                P.emit("DVE", lambda q, psO=psO, cur=cur, nxt=nxt:
                       q.scalar_tensor_tensor(
                           ssb[:, nxt * 128:nxt * 128 + 128],
                           psO[:, 64:192], 1.0,
                           ssb[:, cur * 128:cur * 128 + 128],
                           ALU.bypass, ALU.add),
                       reads=[f"S{cur}", kO] +
                             ([f"ostg{(ci - NCP) % 2}"] if ci >= NCP else []),
                       writes=[f"S{nxt}"])
                P.emit("DVE", lambda q, ci=ci, nxt=nxt, pcf_a=pcf_a:
                       q.tensor_scalar(
                           ssb[:, nxt * 128:nxt * 128 + 128],
                           ssb[:, nxt * 128:nxt * 128 + 128],
                           pcf_a[:, ci:ci + 1], None, ALU.mult),
                       reads=[f"S{nxt}", pkey], writes=[f"S{nxt}"])

            if h8 + 1 < NHD:
                build_prep(h8 + 1)
            NCR = NCA if _P2SUB > 5 else min(NCA, _NCHDBG)
            emit_AB(0)
            for ci in range(NCR):
                if ci + 1 < NCR:
                    emit_AB(ci + 1)
                emit_C(ci)
                for _ in range(3):
                    if prep_q:
                        prep_q.pop(0)()
            while prep_q:
                prep_q.pop(0)()
            _emit_mean_chunk()
            _emit_mean_chunk()
        while mean_jobs:
            _emit_mean_chunk()

        P.emit("DVE", lambda q: q.tensor_reduce(
            wmp[:, 16:17], wmp[:, 0:8], AX.X, ALU.add), reads=["wmp"], writes=["wmp"])
        P.emit("DVE", lambda q: q.tensor_reduce(
            wmp[:, 17:18], wmp[:, 8:16], AX.X, ALU.add), reads=["wmp"], writes=["wmp"])
        for wi in range(2):
            P.emit("PE", lambda q, wi=wi: q.transpose(
                ps[4][0:1, wi * 128:(wi + 1) * 128], wmp[:, 16 + wi:17 + wi],
                c_idenf[:]), reads=["wmp", "c_idenf"], writes=["ps4"])
        P.emit("ACT", lambda q: q.activation(
            u1b[0:1, 0:256], ps[4][0:1, 0:256], AF.Copy, bias=0.0, scale=1.0),
            reads=["ps4"], writes=["u1b"])
        for wi in range(2):
            P.emit("DVE", lambda q, wi=wi: q.tensor_reduce(
                wrow[0:1, wi:wi + 1], u1b[0:1, wi * 128:(wi + 1) * 128],
                AX.X, ALU.add), reads=["u1b"], writes=["wrow"])
        P.emit("DVE", lambda q: q.tensor_scalar(
            wrow[0:1, 2:4], wrow[0:1, 0:2], 1.0 / (4096.0 * 1024.0), 1e-5,
            ALU.mult, ALU.max), reads=["wrow"], writes=["wrow"])
        P.emit("DVE", lambda q: q.tensor_scalar(
            wrow[0:1, 4:6], wrow[0:1, 2:4], 1.0 / 127.0, None, ALU.mult),
            reads=["wrow"], writes=["wrow"])
        for wi in range(2):
            P.emit("DVE", lambda q, wi=wi: q.tensor_scalar(
                wrow[0:1, 6 + 2 * wi:7 + 2 * wi], wrow[0:1, 2 + wi:3 + wi],
                -0.5, None, ALU.mult), reads=["wrow"], writes=["wrow"])
            P.emit("DVE", lambda q, wi=wi: q.tensor_scalar(
                wrow[0:1, 7 + 2 * wi:8 + 2 * wi], wrow[0:1, 2 + wi:3 + wi],
                0.5, None, ALU.mult), reads=["wrow"], writes=["wrow"])
        P.emit("DVE", lambda q: q.memset(lneg[0:1, 0:128], 1.0),
               reads=[], writes=["lneg"])
        P.emit("PE", lambda q: q.matmul(
            ps[4][:, 8:12], lneg[0:1, 0:128], wrow[0:1, 2:6], start=True,
            stop=True,
            skip_group_check=True), reads=["wrow", "lneg"], writes=["ps4"])
        P.emit("PE", lambda q: q.matmul(
            ps[4][:, 12:16], lneg[0:1, 0:128], wrow[0:1, 6:10], start=True,
            stop=True,
            skip_group_check=True), reads=["wrow", "lneg"], writes=["ps4"])
        P.emit("DVE", lambda q: q.tensor_copy(scp[:, 0:4], ps[4][:, 8:12]),
               reads=["ps4"], writes=["scp"])
        P.emit("DVE", lambda q: q.tensor_copy(scp[:, 20:24], ps[4][:, 12:16]),
               reads=["ps4"], writes=["scp"])

        P.barrier()
        tern_jobs = []
        for wi_, dsrc_ in enumerate((dw1t, dw2t)):
            for cch_ in range(8):
                for ss_ in range(8):
                    tern_jobs.append((wi_, cch_, ss_, dsrc_))
        WST = [lneg[:, 3072:3584], lneg[:, 3584:4096]]
        TMP = [wsl[:, 15360:15872], wsl[:, 15872:16384]]

        def _emit_tern():
            if not tern_jobs:
                return
            wi, cch, ss, dsrc = tern_jobs.pop(0)
            idx = wi * 64 + cch * 8 + ss
            stg = WST[idx % 2]
            tmp = TMP[idx % 2]
            skey = f"wst{idx % 2}"
            mkey = f"wsg{idx % 2}"
            if wi == 0:
                wsrc = dsrc.rearrange("(c p) n -> p c n", p=128) \
                    [:, cch, ss * 512:(ss + 1) * 512]
            else:
                wsrc = dsrc.rearrange("(c p) n -> p c n", p=128) \
                    [:, cch * 4 + ss // 2, (ss % 2) * 512:(ss % 2) * 512 + 512]
            alias = ["lneg"] if idx == 0 else []
            P.dma("POOL", "wquant", stg, wsrc, [f"d_w{wi + 1}"], [skey] + alias)
            P.emit("DVE", lambda q, stg=stg, tmp=tmp, wi=wi: q.tensor_scalar(
                tmp, stg, scp[:, 20 + 2 * wi:21 + 2 * wi], None, ALU.is_lt),
                reads=[skey, "scp"], writes=[mkey])
            tq = t1q if wi == 0 else t2q
            tkey = "tq0" if wi == 0 else "tq1"
            P.emit("DVE", lambda q, tq=tq, cch=cch, ss=ss, stg=stg, tmp=tmp,
                   wi=wi: q.scalar_tensor_tensor(
                       tq[:, cch * 4096 + ss * 512:cch * 4096 + ss * 512 + 512],
                       stg, scp[:, 21 + 2 * wi:22 + 2 * wi], tmp,
                       ALU.is_gt, ALU.subtract),
                   reads=[skey, mkey, "scp"], writes=[tkey])

        P.dma("POOL", "wload2", WO.rearrange("p (h n) -> p h n", h=8),
              dwo.rearrange("(h p) n -> p h n", p=128), ["d_wo"], ["wo"])
        P.dma("SP", "bload", cuml[:], db1[:], ["d_b1"], ["cuml"])
        P.dma("POOL", "bload2", b2b[:], db2[:], ["d_b2"], ["b2b"])

        def p4_A(t):
            oslot = t % 2
            otile = OTL[:, oslot * 1024:oslot * 1024 + 1024]
            P.dma("SP", "oload", otile.rearrange("p (h n) -> p h n", h=8),
                  d_o.rearrange("(h p) tt -> p h tt", p=128)
                  [:, :, t * 128:(t + 1) * 128],
                  ["d_o"], [f"otile{oslot}"], k=4)
            mms = []
            for half in range(2):
                for h8 in range(8):
                    mms.append(lambda q, h8=h8, half=half, otile=otile: q.matmul(
                        ps[half][:, 0:512], otile[:, h8 * 128:(h8 + 1) * 128],
                        WO[:, h8 * 1024 + half * 512:h8 * 1024 + half * 512 + 512],
                        start=(h8 == 0), stop=(h8 == 7)))
            P.group("PE", mms, reads=[f"otile{oslot}", "wo"],
                    writes=["ps0", "ps1"])

        p4_A(0)
        for t in range(PT):
            xa = u1b[:]
            P.dma("SP", "xtmload", xa, dxtm[t * 128:(t + 1) * 128, :],
                  ["d_xtm"], ["u1b"], k=4)
            for half in range(2):
                P.emit("DVE", lambda q, half=half, xa=xa: q.scalar_tensor_tensor(
                    remz[:, half * 512:(half + 1) * 512], ps[half][:, 0:512], 1.0,
                    xa[:, half * 512:(half + 1) * 512], ALU.bypass, ALU.add,
                    accum_out=sct[:, 16 + half:17 + half]),
                    reads=[f"ps{half}", "u1b"], writes=["remz", "sct"])
            if t + 1 < PT:
                p4_A(t + 1)
            P.emit("ACT", lambda q: q.activation(
                u1b[:], remz[:], AF.Square, bias=0.0, scale=1.0,
                accum_out=sct[:, 18:19]), reads=["remz"], writes=["u1b", "sct"])
            P.emit("DVE", lambda q: q.scalar_tensor_tensor(
                sct[:, 19:20], sct[:, 16:17], 1.0, sct[:, 17:18],
                ALU.bypass, ALU.add), reads=["sct"], writes=["sct"])
            P.emit("DVE", lambda q: q.tensor_scalar(
                sct[:, 20:21], sct[:, 19:20], 1.0 / D, None, ALU.mult),
                reads=["sct"], writes=["sct"])
            P.emit("DVE", lambda q: q.tensor_tensor(
                sct[:, 21:22], sct[:, 20:21], sct[:, 20:21], ALU.mult),
                reads=["sct"], writes=["sct"])
            P.emit("DVE", lambda q: q.scalar_tensor_tensor(
                sct[:, 22:23], sct[:, 18:19], 1.0 / D, sct[:, 21:22],
                ALU.mult, ALU.subtract), reads=["sct"], writes=["sct"])
            P.emit("ACT", lambda q: q.activation(
                sct[:, 23:24], sct[:, 22:23], AF.Sqrt, bias=cstv[:, 0:1], scale=1.0),
                reads=["sct", "cstv"], writes=["sct"])
            P.emit("DVE", lambda q: q.reciprocal(sct[:, 24:25], sct[:, 23:24]),
                   reads=["sct"], writes=["sct"])
            P.emit("DVE", lambda q: q.tensor_scalar(
                yf32[:], remz[:], sct[:, 20:21], sct[:, 24:25],
                ALU.subtract, ALU.mult), reads=["remz", "sct"], writes=["yf32"])
            P.dma("POOL", "ystore", d_yscr[t * 128:(t + 1) * 128, :], yf32[:],
                  ["yf32"], ["d_yscr"], k=4)
            P.emit("DVE", lambda q: q.tensor_reduce(
                sct[:, 25:26], yf32[:], AX.X, ALU.max, apply_absolute_value=True),
                reads=["yf32"], writes=["sct"])
            P.emit("ACT", lambda q: q.activation(
                u1b[:], yf32[:], AF.Square, bias=0.0, scale=1.0,
                accum_out=sct[:, 26:27]), reads=["yf32"], writes=["u1b", "sct"])
            P.emit("ACT", lambda q: q.activation(
                sct[:, 27:28], sct[:, 26:27], AF.Sqrt, bias=cstv[:, 0:1],
                scale=cstv[:, 1:2]), reads=["sct", "cstv"], writes=["sct"])
            P.emit("DVE", lambda q: q.reciprocal(sct[:, 28:29], sct[:, 27:28]),
                   reads=["sct"], writes=["sct"])
            P.emit("DVE", lambda q: q.tensor_tensor(
                sct[:, 29:30], sct[:, 28:29], sct[:, 25:26], ALU.mult),
                reads=["sct"], writes=["sct"])
            P.emit("DVE", lambda q: q.tensor_scalar(
                sct[:, 30:31], sct[:, 29:30], 1e-5, None, ALU.max),
                reads=["sct"], writes=["sct"])
            P.emit("DVE", lambda q: q.reciprocal(sct[:, 31:32], sct[:, 30:31]),
                   reads=["sct"], writes=["sct"])
            P.emit("DVE", lambda q: q.scalar_tensor_tensor(
                sct[:, 32:33], sct[:, 31:32], 127.0, sct[:, 28:29],
                ALU.mult, ALU.mult), reads=["sct"], writes=["sct"])
            P.emit("DVE", lambda q, t=t: q.scalar_tensor_tensor(
                scp[:, 4 + t:5 + t], sct[:, 30:31], 1.0, scp[:, 2:3],
                ALU.bypass, ALU.mult), reads=["sct"], writes=["scp"])
            P.emit("DVE", lambda q: q.tensor_scalar(
                u1b[:], yf32[:], sct[:, 32:33], None, ALU.mult),
                reads=["yf32", "sct"], writes=["u1b"])
            P.emit("DVE", lambda q: q.tensor_copy(
                yf32[:].bitcast(I32), u1b[:]),
                reads=["u1b"], writes=["yf32"])
            P.emit("DVE", lambda q: q.scalar_tensor_tensor(
                lneg[:, 0:1024], u1b[:], -0.5, yf32[:].bitcast(I32),
                ALU.add, ALU.is_gt),
                reads=["u1b", "yf32"], writes=["lneg"])
            P.emit("DVE", lambda q: q.scalar_tensor_tensor(
                lneg[:, 1024:2048], u1b[:], 0.5, yf32[:].bitcast(I32),
                ALU.add, ALU.is_lt),
                reads=["u1b", "yf32"], writes=["lneg"])
            P.emit("DVE", lambda q: q.scalar_tensor_tensor(
                lneg[:, 2048:3072], yf32[:].bitcast(I32), 1.0, lneg[:, 0:1024],
                ALU.bypass, ALU.add),
                reads=["yf32", "lneg"], writes=["lneg"])
            P.emit("DVE", lambda q: q.tensor_sub(
                N1S[:, 0:1024], lneg[:, 2048:3072], lneg[:, 1024:2048]),
                reads=["lneg"], writes=["n1tm"])
            trs = [(lambda q, cb=cb: q.transpose(
                pt6[:, cb * 128:(cb + 1) * 128], N1S[:, cb * 128:(cb + 1) * 128],
                c_iden[:])) for cb in range(8)]
            P.group("PE", trs, reads=["n1tm", "c_iden"], writes=["pt6"])
            P.emit("ACT", lambda q: q.activation(
                N1S[:, 1024:2048], pt6[:], AF.Copy, bias=0.0, scale=1.0),
                reads=["pt6"], writes=["n1st"])
            P.dma("ACT", "n1store",
                  d_n1scr.rearrange("(c t p) n -> t p c n", c=DC, p=128)
                  [t, :, :, :],
                  N1S[:, 1024:2048].rearrange("p (c n) -> p c n", c=8),
                  ["n1st"], ["d_n1scr"], k=4)
            for _ in range(8):
                _emit_tern()
        while tern_jobs:
            _emit_tern()

        P.barrier()
        WSP = wsl[:, 15360:16384]

        def p5_A(t):
            n1slot = t % 2
            n1a = N1D[:, n1slot * 1024:n1slot * 1024 + 1024]
            ya = YST[:, n1slot * 1024:n1slot * 1024 + 1024]
            P.dma("SP", "n1load", n1a.rearrange("p (c n) -> p c n", c=8),
                  d_n1scr.rearrange("(c t p) n -> t p c n", c=DC, p=128)
                  [t, :, :, :],
                  ["d_n1scr"], [f"n1a{n1slot}"], k=4)
            P.dma("SP", "yload", ya, d_yscr[t * 128:(t + 1) * 128, :],
                  ["d_yscr"], [f"ya{n1slot}"], k=4)
            for s in range(8):
                bank = s % 4
                mms = [(lambda q, c=c, s=s, bank=bank, n1a=n1a: q.matmul(
                    ps[bank][:, 0:512], n1a[:, c * 128:(c + 1) * 128],
                    t1q[:, c * 4096 + s * 512:c * 4096 + (s + 1) * 512],
                    start=(c == 0), stop=(c == DC - 1))) for c in range(DC)]
                P.group("PE", mms, reads=[f"n1a{n1slot}", "tq0"],
                        writes=[f"ps{bank}"])
                P.emit("DVE", lambda q, bank=bank, s=s, t=t:
                       q.scalar_tensor_tensor(
                           lneg[:, s * 512:(s + 1) * 512], ps[bank][:, 0:512],
                           scp[:, 4 + t:5 + t], cuml[:, s * 512:(s + 1) * 512],
                           ALU.mult, ALU.add),
                       reads=[f"ps{bank}", "scp", "cuml"],
                       writes=["lneg", "wst0", "wst1"])
                P.emit("ACT", lambda q, s=s: q.activation(
                    lneg[:, s * 512:(s + 1) * 512],
                    lneg[:, s * 512:(s + 1) * 512],
                    AF.Gelu, bias=0.0, scale=1.0),
                    reads=["lneg"], writes=["lneg"])

        def p5_B(t):
            for qq in range(4):
                qsl = slice(qq * 1024, (qq + 1) * 1024)
                P.emit("DVE", lambda q, qsl=qsl, qq=qq: q.tensor_reduce(
                    sct[:, 36 + qq:37 + qq], lneg[:, qsl], AX.X, ALU.max,
                    apply_absolute_value=True),
                    reads=["lneg"], writes=["sctB"])
                P.emit("ACT", lambda q, qsl=qsl, qq=qq: q.activation(
                    remz[:], lneg[:, qsl], AF.Square, bias=0.0, scale=1.0,
                    accum_out=sct[:, 60 + qq:61 + qq]),
                    reads=["lneg"], writes=["remz", "sctB"])
            P.emit("DVE", lambda q: q.tensor_reduce(
                sct[:, 42:43], sct[:, 36:40], AX.X, ALU.max),
                reads=["sctB"], writes=["sctB"])
            P.emit("DVE", lambda q: q.tensor_reduce(
                sct[:, 45:46], sct[:, 60:64], AX.X, ALU.add),
                reads=["sctB"], writes=["sctB"])
            P.emit("ACT", lambda q: q.activation(
                sct[:, 46:47], sct[:, 45:46], AF.Sqrt, bias=cstv[:, 0:1],
                scale=cstv[:, 2:3]), reads=["sctB", "cstv"], writes=["sctB"])
            P.emit("DVE", lambda q: q.reciprocal(sct[:, 47:48], sct[:, 46:47]),
                   reads=["sctB"], writes=["sctB"])
            P.emit("DVE", lambda q: q.tensor_tensor(
                sct[:, 48:49], sct[:, 47:48], sct[:, 42:43], ALU.mult),
                reads=["sctB"], writes=["sctB"])
            P.emit("DVE", lambda q: q.tensor_scalar(
                sct[:, 49:50], sct[:, 48:49], 1e-5, None, ALU.max),
                reads=["sctB"], writes=["sctB"])
            P.emit("DVE", lambda q: q.reciprocal(sct[:, 50:51], sct[:, 49:50]),
                   reads=["sctB"], writes=["sctB"])
            P.emit("DVE", lambda q: q.scalar_tensor_tensor(
                sct[:, 51:52], sct[:, 50:51], 127.0, sct[:, 47:48],
                ALU.mult, ALU.mult), reads=["sctB"], writes=["sctB"])
            P.emit("DVE", lambda q: q.scalar_tensor_tensor(
                sct[:, 52:53], sct[:, 49:50], 1.0, scp[:, 3:4],
                ALU.bypass, ALU.mult), reads=["sctB", "scp"], writes=["sctB"])
            for qq in range(4):
                qsl = slice(qq * 1024, (qq + 1) * 1024)
                P.emit("DVE", lambda q, qsl=qsl: q.tensor_scalar(
                    u1b[:], lneg[:, qsl], sct[:, 51:52], None, ALU.mult),
                    reads=["lneg", "sctB"], writes=["u1b"])
                P.emit("DVE", lambda q: q.tensor_copy(
                    yf32[:].bitcast(I32), u1b[:]),
                    reads=["u1b"], writes=["yf32"])
                P.emit("DVE", lambda q: q.scalar_tensor_tensor(
                    remz[:], u1b[:], -0.5, yf32[:].bitcast(I32),
                    ALU.add, ALU.is_gt),
                    reads=["u1b", "yf32"], writes=["remz"])
                P.emit("DVE", lambda q: q.scalar_tensor_tensor(
                    WSP, u1b[:], 0.5, yf32[:].bitcast(I32),
                    ALU.add, ALU.is_lt),
                    reads=["u1b", "yf32"], writes=["wspD", "wsg0", "wsg1"])
                P.emit("DVE", lambda q: q.scalar_tensor_tensor(
                    u1b[:], yf32[:].bitcast(I32), 1.0, remz[:],
                    ALU.bypass, ALU.add),
                    reads=["yf32", "remz"], writes=["u1b"])
                P.emit("DVE", lambda q, qsl=qsl: q.tensor_sub(
                    N2TM[:, qsl], u1b[:], WSP),
                    reads=["u1b", "wspD"], writes=["n2tm"])
            for bb in range(4):
                pv = pt6 if bb % 2 == 0 else pt7
                pk = f"pt{6 + bb % 2}"
                trs = [(lambda q, cb=bb * 8 + kk, kk=kk, pv=pv: q.transpose(
                    pv[:, kk * 128:(kk + 1) * 128],
                    N2TM[:, cb * 128:(cb + 1) * 128], c_iden[:]))
                    for kk in range(8)]
                P.group("PE", trs, reads=["n2tm", "c_iden"], writes=[pk])
                P.emit("ACT", lambda q, bb=bb, pv=pv: q.activation(
                    N2F[:, bb * 1024:(bb + 1) * 1024], pv[:],
                    AF.Copy, bias=0.0, scale=1.0),
                    reads=[pk], writes=["n2F"])

        def p5_C(t):
            n1slot = t % 2
            ya = YST[:, n1slot * 1024:n1slot * 1024 + 1024]
            mms = []
            for c2 in range(FC):
                for half in range(2):
                    mms.append(lambda q, c2=c2, half=half: q.matmul(
                        ps[4 + half][:, 0:512], N2F[:, c2 * 128:(c2 + 1) * 128],
                        t2q[:, c2 * 1024 + half * 512:
                            c2 * 1024 + (half + 1) * 512],
                        start=(c2 == 0), stop=(c2 == FC - 1)))
            P.group("PE", mms, reads=["n2F", "tq1"], writes=["ps4", "ps5"])
            for half in range(2):
                P.emit("DVE", lambda q, half=half: q.scalar_tensor_tensor(
                    remz[:, half * 512:(half + 1) * 512],
                    ps[4 + half][:, 0:512],
                    sct[:, 52:53], b2b[:, half * 512:(half + 1) * 512],
                    ALU.mult, ALU.add),
                    reads=[f"ps{4 + half}", "sctB", "b2b"], writes=["remz"])
            P.emit("DVE", lambda q, ya=ya, n1slot=n1slot: q.scalar_tensor_tensor(
                yf32[:], remz[:], 1.0, ya, ALU.bypass, ALU.add,
                accum_out=sct[:, 53:54]),
                reads=["remz", f"ya{n1slot}"], writes=["yf32", "sctC"])
            P.emit("ACT", lambda q: q.activation(
                u1b[:], yf32[:], AF.Square, bias=0.0, scale=1.0,
                accum_out=sct[:, 54:55]), reads=["yf32"],
                writes=["u1b", "sctC"])
            P.emit("DVE", lambda q: q.tensor_scalar(
                sct[:, 55:56], sct[:, 53:54], 1.0 / D, None, ALU.mult),
                reads=["sctC"], writes=["sctC"])
            P.emit("DVE", lambda q: q.tensor_tensor(
                sct[:, 56:57], sct[:, 55:56], sct[:, 55:56], ALU.mult),
                reads=["sctC"], writes=["sctC"])
            P.emit("DVE", lambda q: q.scalar_tensor_tensor(
                sct[:, 57:58], sct[:, 54:55], 1.0 / D, sct[:, 56:57],
                ALU.mult, ALU.subtract), reads=["sctC"], writes=["sctC"])
            P.emit("ACT", lambda q: q.activation(
                sct[:, 58:59], sct[:, 57:58], AF.Sqrt, bias=cstv[:, 0:1],
                scale=1.0),
                reads=["sctC", "cstv"], writes=["sctC"])
            P.emit("DVE", lambda q: q.reciprocal(sct[:, 59:60], sct[:, 58:59]),
                   reads=["sctC"], writes=["sctC"])
            P.emit("DVE", lambda q: q.tensor_scalar(
                remz[:], yf32[:], sct[:, 55:56], sct[:, 59:60],
                ALU.subtract, ALU.mult),
                reads=["yf32", "sctC"], writes=["remz"])
            P.dma("POOL", "outstore", dout[t * 128:(t + 1) * 128, :], remz[:],
                  ["remz"], ["d_out"], k=4)

        p5_A(0)
        for t in range(PT):
            p5_B(t)
            if t + 1 < PT:
                p5_A(t + 1)
            p5_C(t)

        P.barrier()
    return nc


def _consts():
    import ml_dtypes
    bf = ml_dtypes.bfloat16
    return {
        "iden": np.eye(128, dtype=np.float32).astype(bf),
        "idenf": np.eye(128, dtype=np.float32),
        "tri": np.vstack([np.triu(np.ones((64, 64), dtype=np.float32))] * 2).astype(bf),
        "onescol": np.ones((128, 1), dtype=np.float32).astype(bf),
    }


def kernel(L0_attn_x, L0_attn_Wq, L0_attn_Wk, L0_attn_Wv, L0_attn_Wg,
           L0_attn_Wo, L0_ffn_w1, L0_ffn_b1, L0_ffn_w2, L0_ffn_b2):
    x = np.ascontiguousarray(np.asarray(L0_attn_x, dtype=np.float32))
    Wq = np.ascontiguousarray(np.asarray(L0_attn_Wq, dtype=np.float32))
    Wk = np.ascontiguousarray(np.asarray(L0_attn_Wk, dtype=np.float32))
    Wv = np.ascontiguousarray(np.asarray(L0_attn_Wv, dtype=np.float32))
    Wg = np.ascontiguousarray(np.asarray(L0_attn_Wg, dtype=np.float32))
    Wo = np.ascontiguousarray(np.asarray(L0_attn_Wo, dtype=np.float32))
    w1t = np.ascontiguousarray(np.asarray(L0_ffn_w1, dtype=np.float32).T)
    w2t = np.ascontiguousarray(np.asarray(L0_ffn_w2, dtype=np.float32).T)
    b1 = np.ascontiguousarray(
        np.broadcast_to(np.asarray(L0_ffn_b1, dtype=np.float32), (128, 4096)))
    b2 = np.ascontiguousarray(
        np.broadcast_to(np.asarray(L0_ffn_b2, dtype=np.float32), (128, 1024)))
    B, T, D = x.shape
    TW = T // 2
    PFX = 256
    cst = _consts()
    in_maps = []
    for c in range(8):
        b, par = c // 2, c % 2
        xT = x[b].T
        if par == 0:
            xdm = np.hstack([np.zeros((D, PFX), np.float32), xT[:, 0:TW]])
        else:
            xdm = xT[:, TW - PFX:T]
        m = {
            "xdm": np.ascontiguousarray(xdm),
            "xtm": np.ascontiguousarray(x[b, par * TW:(par + 1) * TW]),
            "wq": Wq, "wk": Wk, "wg": Wg, "wv": Wv,
            "wo": Wo, "w1t": w1t, "w2t": w2t, "b1": b1, "b2": b2,
        }
        m.update(cst)
        in_maps.append(m)
    nc = build_program(TW=TW, PFX=PFX)
    res = run_bass_kernel_spmd(nc, in_maps, core_ids=list(range(8)))
    out = np.empty((B, T, D), dtype=np.float32)
    for c in range(8):
        b, par = c // 2, c % 2
        out[b, par * TW:(par + 1) * TW] = np.asarray(res.results[c]["out"])
    return out
